# version 34
# baseline (speedup 1.0000x reference)
import numpy as np
from contextlib import ExitStack

import concourse.bass as bass
import concourse.mybir as mybir
from concourse.bass_utils import run_bass_kernel_spmd

F32 = mybir.dt.float32
BF16 = mybir.dt.bfloat16
AF = mybir.ActivationFunctionType
OP = mybir.AluOpType
AX = mybir.AxisListType.X

T, D = 2048, 1024
H, DH = 16, 64
TC = T // 128
KC = D // 128
NHP = H // 2
G = 2
NG = TC // G
LN_EPS = 1e-5
ATT_EPS = 1e-6

ENGINES = ("tensor", "vector", "scalar", "gpsimd", "sync")


class Sem:
    __slots__ = ("h", "n")

    def __init__(self, h):
        self.h = h
        self.n = 0


class Rec:
    def __init__(self):
        self.q = {e: [] for e in ENGINES}

    def wait(self, eng, sem, val):
        if val is not None and val > 0:
            self.q[eng].append(("wait_ge", (sem.h, val), {}, None, 0))

    def op(self, eng, meth, *args, inc=None, k=1, **kw):
        if inc is not None:
            inc.n += k
        self.q[eng].append((meth, args, kw, inc, k))
        return inc.n if inc is not None else None

    def drain(self, eng):
        self.q[eng].append(("drain", (), {}, None, 0))


class _Cut(Exception):
    pass


def build_program(upto=None):
    def cut(label):
        if upto is not None and label == upto:
            raise _Cut()
    nc = bass.Bass("TRN2", target_bir_lowering=False)
    x_d = nc.dram_tensor("x", [T, D], F32, kind="ExternalInput")
    wa_d = nc.dram_tensor("w_attn", [D, 3 * D], F32, kind="ExternalInput")
    wp_d = nc.dram_tensor("w_proj", [D, D], F32, kind="ExternalInput")
    gw_d = nc.dram_tensor("gate_w", [8, D], F32, kind="ExternalInput")
    gb_d = nc.dram_tensor("gate_b", [1, 8], F32, kind="ExternalInput")
    ew_d = nc.dram_tensor("expert_w", [8, D, D], F32, kind="ExternalInput")
    eb_d = nc.dram_tensor("expert_b", [8, D], F32, kind="ExternalInput")
    out_d = nc.dram_tensor("out", [T, D], F32, kind="ExternalOutput")

    ctx = ExitStack()
    sb = lambda name, shape, dt: ctx.enter_context(nc.sbuf_tensor(name, shape, dt))
    ps = lambda name, shape, dt: ctx.enter_context(nc.psum_tensor(name, shape, dt))
    sem = lambda name: Sem(ctx.enter_context(nc.semaphore(name)))

    r = Rec()
    E = {}
    _rg = {"base": 0, "rows": 128}

    def wait_ev(eng, ev):
        if ev is not None:
            s, v = ev
            r.wait(eng, s, v)

    def rg(base, rows):
        if rows < 128 and _rg["rows"] < 128 and (_rg["base"], _rg["rows"]) != (base, rows):
            raise AssertionError(f"row-group hazard: {_rg} -> {(base, rows)}")
        _rg["base"], _rg["rows"] = base, rows

    sX0 = sem("sX0")
    sX0p = [sem("sX0a"), sem("sX0b")]
    sX1 = sem("sX1")
    sGWl = sem("sGWl")
    sGBl = sem("sGBl")
    sEBl = sem("sEBl")
    sWA = sem("sWA")
    sWAp = [sem(f"sWAp{i}") for i in range(4)]
    sWP = sem("sWP")
    sEW = [sem("sEW0"), sem("sEW1")]
    sTN = sem("sTN")
    sVE = sem("sVE")
    sSC = sem("sSC")
    sGP = sem("sGP")
    sST = sem("sST")

    xres = sb("xres", [128, TC * 1024], F32)
    idb = sb("idb", [128, 128], BF16)
    idf = sb("idf", [128, 128], F32)
    maskT = sb("maskT", [128, 256], F32)
    onesf = sb("onesf", [128, 128], F32)
    saug = sb("saug", [128, NHP * 65], F32)
    saugbf = sb("saugbf", [128, NHP * 65], BF16)
    statsA = sb("statsA", [128, TC * 8], F32)
    statsB = sb("statsB", [128, TC * 8], F32)
    gw_s = sb("gw_s", [8, 1024], F32)
    gb_s = sb("gb_s", [1, 8], F32)
    eb_s = sb("eb_s", [8, 1024], F32)
    gwT = sb("gwT", [128, 64], F32)
    expbfm = sb("expbfm", [128, 64], F32)
    pooledfm = sb("pooledfm", [128, 8], F32)
    pooled_sb = sb("pooled_sb", [1, 1024], F32)
    gatesb = sb("gatesb", [128, 8], F32)
    small = sb("small", [1, 32], F32)
    lg_sb = sb("lg_sb", [1, 8], F32)
    tt_sb = sb("tt_sb", [1, 16], F32)
    sqbuf = sb("sqbuf", [128, 1024], F32)
    r2pre = sb("r2pre", [128, 2], F32)
    r2 = sb("r2", [128, 2], F32)
    epscol = sb("epscol", [128, 1], F32)
    ewn0 = sb("ewn0", [128, 4096], BF16)

    for q in range(2):
        srcq = x_d[512 * q:512 * (q + 1), :].rearrange("(c p) d -> p c d", p=128)
        dstq = xres[:, 4096 * q:4096 * (q + 1)].rearrange("p (c d) -> p c d", d=1024)
        r.op("sync", "dma_start", inc=sX0p[q], k=16, out=dstq, in_=srcq)
    src1 = x_d[1024:2048, :].rearrange("(c p) d -> p c d", p=128)
    dst1 = xres[:, 8192:16384].rearrange("p (c d) -> p c d", d=1024)
    r.op("sync", "dma_start", inc=sX1, k=16, out=dst1, in_=src1)
    r.op("sync", "dma_start", inc=sGWl, k=16, out=gw_s[:, :], in_=gw_d[:, :])
    r.op("sync", "dma_start", inc=sGBl, k=16, out=gb_s[:, :], in_=gb_d[:, :])
    r.op("sync", "dma_start", inc=sEBl, k=16, out=eb_s[:, :], in_=eb_d[:, :])

    r.op("gpsimd", "memset", idb[:, :], 0.0)
    r.drain("gpsimd")
    r.op("gpsimd", "affine_select", out=idb[:, :], in_=idb[:, :], compare_op=OP.not_equal,
         fill=1.0, base=0, pattern=[[-1, 128]], channel_multiplier=1)
    r.op("gpsimd", "memset", idf[:, :], 0.0)
    r.drain("gpsimd")
    r.op("gpsimd", "affine_select", out=idf[:, :], in_=idf[:, :], compare_op=OP.not_equal,
         fill=1.0, base=0, pattern=[[-1, 128]], channel_multiplier=1)
    r.op("gpsimd", "memset", maskT[:, :], 0.0)
    r.drain("gpsimd")
    for h in range(2):
        r.op("gpsimd", "affine_select", out=maskT[:, 128 * h:128 * (h + 1)],
             in_=maskT[:, 128 * h:128 * (h + 1)], compare_op=OP.is_gt, fill=1.0,
             base=0, pattern=[[-1, 128]], channel_multiplier=1)
    r.op("gpsimd", "memset", onesf[:, :], 1.0)
    r.op("gpsimd", "memset", epscol[:, :], LN_EPS)
    r.op("gpsimd", "memset", saug[:, :], 0.0)
    E[("const",)] = r.op("gpsimd", "memset", saugbf[:, :], 0.0, inc=sGP)

    try:
      with ExitStack() as actx:
        asb = lambda name, shape, dt: actx.enter_context(nc.sbuf_tensor(name, shape, dt))
        aps = lambda name, shape, dt: actx.enter_context(nc.psum_tensor(name, shape, dt))

        wattn = asb("wattn", [128, KC * 3072], BF16)
        wproj = asb("wproj", [128, KC * 1024], BF16)
        xn1 = asb("xn1", [128, G * 1024], BF16)
        xfm = asb("xfm", [128, 2 * KC * 128 * G], BF16)
        qfm = asb("qfm", [128, 2 * 8 * 128 * G], BF16)
        kfm = asb("kfm", [128, 2 * 8 * 128 * G], BF16)
        vaug = asb("vaug", [128, G * 1040], BF16)
        ytok = asb("ytok", [128, G * 1024], BF16)
        yfm = asb("yfm", [128, KC * 128 * G], BF16)
        elumin = asb("elumin", [128, 512], F32)
        eluex = asb("eluex", [128, 512], F32)
        at_sb = asb("at_sb", [128, 2 * 256], BF16)
        ktok = asb("ktok", [128, 2 * 128], BF16)

        pbulk = [aps("pbulk0", [128, 512], F32), aps("pbulk1", [128, 512], F32)]
        pT = aps("pT", [128, 1024], BF16)
        pSTg = [aps("pSTa", [128, 256], F32), aps("pSTb", [128, 256], F32)]
        pND = aps("pND", [128, 130], F32)
        pKT = aps("pKT", [128, 128], BF16)
        pDS = aps("pDS", [128, 65], F32)
        ndbuf = asb("ndbuf", [128, 2 * 130], F32)

        for q4 in range(4):
            r.op("gpsimd", "dma_start", inc=sWAp[q4], k=16,
                 out=wattn[:, 3072 * 2 * q4:3072 * 2 * (q4 + 1)]
                 .rearrange("p (kc o) -> p kc o", o=3072),
                 in_=wa_d[256 * q4:256 * (q4 + 1), :]
                 .rearrange("(kc p) o -> p kc o", p=128))
        E[("wproj",)] = r.op(
            "gpsimd", "dma_start", inc=sWP, k=16,
            out=wproj[:, :].rearrange("p (kc o) -> p kc o", o=1024),
            in_=wp_d[:, :].rearrange("(kc p) o -> p kc o", p=128))
        E[("ewld", 0)] = r.op(
            "gpsimd", "dma_start", inc=sEW[0], k=16,
            out=ewn0[:, :].rearrange("p (oc d) -> p oc d", d=1024),
            in_=ew_d[0, 0:512, :].rearrange("(oc p) d -> p oc d", p=128))
        r.wait("gpsimd", sGP, E[("const",)])
        va = vaug[:, :]
        ones_ap = bass.AP(va.tensor, va.offset + 64, [list(va.ap[0]), [1040, G], [65, 16]])
        E[("vones",)] = r.op("gpsimd", "memset", ones_ap, 1.0, inc=sGP)

        def ln(c, stats, xn_dst, vec_apply=False):
            s0 = 8 * c
            xc = xres[:, 1024 * c:1024 * (c + 1)]
            t = []
            dstf = xn_dst if callable(xn_dst) else (lambda: xn_dst)

            def t0():
                r.wait("scalar", sGP, E[("const",)])
                r.drain("scalar")
                r.op("scalar", "activation", sqbuf[:, :], xc, AF.Square,
                     accum_out=stats[:, s0 + 1:s0 + 2])
            t.append(t0)

            def t1():
                r.drain("scalar")
                r.op("scalar", "activation", sqbuf[:, :], xc, AF.Identity,
                     accum_out=stats[:, s0:s0 + 1])
            t.append(t1)

            def t2():
                r.drain("scalar")
                r.op("scalar", "activation", stats[:, s0 + 2:s0 + 3],
                     stats[:, s0:s0 + 1], AF.Identity, scale=-1.0 / 1024)
                r.op("scalar", "activation", stats[:, s0 + 3:s0 + 4],
                     stats[:, s0 + 1:s0 + 2], AF.Identity, scale=1.0 / 1024,
                     bias=epscol[:, :])
            t.append(t2)

            def t3():
                r.drain("scalar")
                r.op("scalar", "activation", stats[:, s0 + 4:s0 + 5],
                     stats[:, s0 + 2:s0 + 3], AF.Square)
            t.append(t3)

            def t4():
                r.drain("scalar")
                r.op("scalar", "activation", stats[:, s0 + 7:s0 + 8],
                     stats[:, s0 + 4:s0 + 5], AF.Identity, scale=-1.0,
                     bias=stats[:, s0 + 3:s0 + 4])
            t.append(t4)

            def t5():
                r.drain("scalar")
                r.op("scalar", "activation", stats[:, s0 + 5:s0 + 6],
                     stats[:, s0 + 7:s0 + 8], AF.Ln)
            t.append(t5)

            def t6():
                r.drain("scalar")
                r.op("scalar", "activation", stats[:, s0 + 5:s0 + 6],
                     stats[:, s0 + 5:s0 + 6], AF.Exp, scale=-0.5)
            t.append(t6)

            def t7():
                r.drain("scalar")
                ev = r.op("scalar", "activation", stats[:, s0 + 6:s0 + 7],
                          stats[:, s0 + 2:s0 + 3], AF.Identity,
                          scale=stats[:, s0 + 5:s0 + 6],
                          inc=(sSC if vec_apply else None))
                if vec_apply:
                    E[("lnst", c)] = ev
            t.append(t7)

            def t8():
                if vec_apply:
                    r.wait("vector", sSC, E[("lnst", c)])
                    r.drain("vector")
                    E[("xn1", c)] = r.op("vector", "tensor_scalar", dstf(), xc,
                                         stats[:, s0 + 5:s0 + 6],
                                         stats[:, s0 + 6:s0 + 7],
                                         OP.mult, OP.add, inc=sVE)
                else:
                    r.drain("scalar")
                    E[("xn1", c)] = r.op("scalar", "activation", dstf(), xc, AF.Identity,
                                         bias=stats[:, s0 + 6:s0 + 7],
                                         scale=stats[:, s0 + 5:s0 + 6], inc=sSC)
            t.append(t8)
            return t

        def emit_ds(pc, php, pj):
            r.wait("tensor", sSC, E[("ktok", pc, php)])
            saprev = E.get(("sadd", pc - 1 if php == 0 else pc, php - 1 if php > 0 else 7))
            if saprev:
                r.wait("tensor", sVE, saprev)
            pslot = php % 2
            rg(0, 128)
            r.op("tensor", "matmul", pDS[0:64, :],
                 ktok[:, 128 * pslot:128 * pslot + 64],
                 vaug[:, 1040 * pj + 65 * 2 * php:1040 * pj + 65 * (2 * php + 1)],
                 start=True, stop=True, tile_position=(0, 0))
            E[("dsmm", pc, php)] = r.op(
                "tensor", "matmul", pDS[64:128, :],
                ktok[:, 128 * pslot + 64:128 * (pslot + 1)],
                vaug[:, 1040 * pj + 65 * (2 * php + 1):1040 * pj + 65 * (2 * php + 2)],
                start=True, stop=True, tile_position=(0, 64), inc=sTN)
            if php == NHP - 1:
                E[("attn_done", pc)] = E[("dsmm", pc, php)]

            def upd():
                r.wait("vector", sTN, E[("dsmm", pc, php)])
                r.drain("vector")
                E[("sadd", pc, php)] = r.op(
                    "vector", "tensor_add", saug[:, 65 * php:65 * (php + 1)],
                    saug[:, 65 * php:65 * (php + 1)], pDS[:, :], inc=sVE)
                r.wait("scalar", sVE, E[("sadd", pc, php)])
                E[("sbf", pc, php)] = r.op(
                    "scalar", "copy", saugbf[:, 65 * php:65 * (php + 1)],
                    saug[:, 65 * php:65 * (php + 1)], inc=sSC)
            defer(1, upd)

        DEF = {}
        BI = {"n": 0}

        DEFP = {}

        def defer(k, fn):
            DEF.setdefault(BI["n"] + k, []).append(fn)

        def defer_post(k, fn):
            DEFP.setdefault(BI["n"] + k, []).append(fn)

        def flush():
            for fn in DEF.pop(BI["n"], []):
                fn()

        def flush_post():
            for fn in DEFP.pop(BI["n"], []):
                fn()

        def flush_all():
            for bi in sorted(set(DEF) | set(DEFP)):
                for fn in DEF.pop(bi, []):
                    fn()
                for fn in DEFP.pop(bi, []):
                    fn()

        def emit_lns(g, spread):
            for j in range(G):
                c = G * g + j
                thunks = ln(c, statsA, xn1[:, 1024 * j:1024 * (j + 1)])

                def first(j=j, c=c, t0=thunks[0]):
                    sx = sX0p[c // 4] if c < 8 else sX1
                    r.wait("scalar", sx, 16)
                    if c >= G:
                        r.wait("scalar", sTN, E[("xtr", (c - G) // G, j)])
                    t0()
                thunks[0] = first
                if spread:
                    for k, fn in enumerate(thunks):
                        defer(1 + 2 * j + k // 3, fn)
                else:
                    for fn in thunks:
                        fn()

        def emit_xtr(g, j, inline=False):
            par = g % 2
            c = G * g + j
            r.wait("tensor", sSC, E[("xn1", c)])
            if g == 0 and j == 0:
                r.wait("tensor", sGP, E[("const",)])
            prev = E.get(("ptfree",))
            if prev:
                r.wait("tensor", sVE, prev)
            for kc in range(8):
                rg(0, 128)
                r.op("tensor", "transpose", pT[:, 128 * kc:128 * (kc + 1)],
                     xn1[:, 1024 * j + 128 * kc:1024 * j + 128 * (kc + 1)], idb[:, :],
                     inc=(sTN if kc == 7 else None))
            E[("xtr", g, j)] = sTN.n

            def evac():
                r.wait("vector", sTN, E[("xtr", g, j)])
                if g >= 2 and j == 0:
                    r.wait("vector", sTN, E[("vmm", G * (g - 2) + 1, 1)])
                xf = xfm[:, :]
                dst = bass.AP(xf.tensor, xf.offset + 2048 * par + 128 * j,
                              [list(xf.ap[0]), [256, 8], [1, 128]])
                E[("ptfree",)] = r.op("vector", "tensor_copy", dst, pT[:, :], inc=sVE)
                if j == 1:
                    E[("xfm", g)] = sVE.n
            if inline:
                evac()
            else:
                defer(1, evac)

        def emit_qk(g, p, inline=False):
            par = g % 2
            if p == 0:
                r.wait("tensor", sVE, E[("xfm", g)])
            bank = pbulk[p % 2]
            wait_ev("tensor", E.get(("pbulkfree", p % 2)))
            is_q = p < 4
            base = 0 if is_q else 1024
            for sub in range(2):
                oc = (2 * p + sub) % 8
                for kc in range(KC):
                    if g == 0 and p == 0 and sub == 0:
                        r.wait("tensor", sWAp[kc // 2], 16)
                    rg(0, 128)
                    r.op("tensor", "matmul",
                         bank[:, 256 * sub:256 * (sub + 1)],
                         wattn[:, 3072 * kc + base + 128 * oc:3072 * kc + base + 128 * (oc + 1)],
                         xfm[:, 2048 * par + 256 * kc:2048 * par + 256 * (kc + 1)],
                         start=(kc == 0 and sub == 0), stop=(kc == KC - 1 and sub == 1),
                         inc=(sTN if (kc == KC - 1 and sub == 1) else None))
            E[("qkmm", g, p)] = sTN.n

            def minexp():
                r.wait("scalar", sTN, E[("qkmm", g, p)])
                eprev = E.get(("qkem", g, p - 1)) if p > 0 else (E.get(("qkem", g - 1, 7)) if g > 0 else None)
                if eprev:
                    r.wait("scalar", sVE, eprev)
                E[("qkexp", g, p)] = r.op("scalar", "activation", eluex[:, :],
                                          bank[:, :], AF.Exp, inc=sSC)

            def stt():
                r.wait("vector", sSC, E[("qkexp", g, p)])
                r.drain("vector")
                E[("qkem", g, p)] = r.op("vector", "tensor_scalar", elumin[:, :],
                                         eluex[:, :], 1.0, None, OP.min, inc=sVE)
                if g >= 2 and p == 0:
                    r.wait("vector", sTN, E[("ndmm", G * (g - 2) + 1, NHP - 1)])
                dstbuf = qfm if is_q else kfm
                off = 2048 * par + (512 * p if is_q else 512 * (p - 4))
                r.drain("vector")
                ev = r.op("vector", "scalar_tensor_tensor", dstbuf[:, off:off + 512],
                          bank[:, :], 0.0, elumin[:, :], OP.max, OP.add, inc=sVE)
                E[("pbulkfree", p % 2)] = (sVE, ev)
                E[("qkstt", g, p)] = ev
            if inline:
                minexp()
                stt()
            else:
                defer(1, minexp)
                defer_post(1, stt)

        def emit_v(g, j, oh, inline=False):
            par = g % 2
            c = G * g + j
            bank = pbulk[oh]
            wait_ev("tensor", E.get(("pbulkfree", oh)))
            if g == 0 and j == 0 and oh == 0:
                r.wait("tensor", sGP, E[("vones",)])
            for kc in range(KC):
                rg(0, 128)
                r.op("tensor", "matmul", bank[:, :],
                     xfm[:, 2048 * par + 256 * kc + 128 * j:
                         2048 * par + 256 * kc + 128 * (j + 1)],
                     wattn[:, 3072 * kc + 2048 + 512 * oh:3072 * kc + 2048 + 512 * (oh + 1)],
                     start=(kc == 0), stop=(kc == KC - 1),
                     inc=(sTN if kc == KC - 1 else None))
            E[("vmm", c, oh)] = sTN.n

            def evac():
                r.wait("scalar", sTN, E[("vmm", c, oh)])
                if c >= G and oh == 0:
                    r.wait("scalar", sTN, E[("attn_done", c - G)])
                vv = vaug[:, :]
                dst = bass.AP(vv.tensor, vv.offset + 1040 * j + 65 * 8 * oh,
                              [list(vv.ap[0]), [65, 8], [1, 64]])
                src = bass.AP(bank.ap().tensor, bank.ap().offset,
                              [list(bank.ap().ap[0]), [64, 8], [1, 64]])
                ev = r.op("scalar", "copy", dst, src, inc=sSC)
                E[("pbulkfree", oh)] = (sSC, ev)
                E[("vevS", c, oh)] = ev
            if inline:
                evac()
            else:
                defer(1, evac)

        def emit_ytr(g, j, c, inline=False):
            r.wait("tensor", sVE, E[("yev", c, NHP - 1)])
            prev = E.get(("ptfree",))
            if prev:
                r.wait("tensor", sVE, prev)
            for kc in range(8):
                rg(0, 128)
                r.op("tensor", "transpose", pT[:, 128 * kc:128 * (kc + 1)],
                     ytok[:, 1024 * j + 128 * kc:1024 * j + 128 * (kc + 1)], idb[:, :],
                     inc=(sTN if kc == 7 else None))
            E[("ytr", c)] = sTN.n

            def evac():
                r.wait("vector", sTN, E[("ytr", c)])
                if c >= G:
                    r.wait("vector", sTN, E[("projmm", c - G, 1)])
                yf = yfm[:, :]
                dst = bass.AP(yf.tensor, yf.offset + 128 * j,
                              [list(yf.ap[0]), [256, 8], [1, 128]])
                E[("ptfree",)] = r.op("vector", "tensor_copy", dst, pT[:, :], inc=sVE)
                E[("yfm", c)] = sVE.n
            if inline:
                evac()
            else:
                defer(1, evac)

        def emit_proj(g, j, c, oh, inline=False):
            bank = pbulk[oh]
            wait_ev("tensor", E.get(("pbulkfree", oh)))
            if c == 0 and oh == 0:
                r.wait("tensor", sWP, 16)
            r.wait("tensor", sVE, E[("yfm", c)])
            for kc in range(KC):
                rg(0, 128)
                r.op("tensor", "matmul", bank[:, :],
                     yfm[:, 256 * kc + 128 * j:256 * kc + 128 * (j + 1)],
                     wproj[:, 1024 * kc + 512 * oh:1024 * kc + 512 * (oh + 1)],
                     start=(kc == 0), stop=(kc == KC - 1),
                     inc=(sTN if kc == KC - 1 else None))
            E[("projmm", c, oh)] = sTN.n

            def resid():
                r.wait("vector", sTN, E[("projmm", c, oh)])
                r.drain("vector")
                ev = r.op("vector", "tensor_add",
                          xres[:, 1024 * c + 512 * oh:1024 * c + 512 * (oh + 1)],
                          xres[:, 1024 * c + 512 * oh:1024 * c + 512 * (oh + 1)],
                          bank[:, :], inc=sVE)
                E[("pbulkfree", oh)] = (sVE, ev)
                if oh == 1:
                    E[("res1", c)] = ev
            if inline:
                resid()
            else:
                defer(1, resid)

        def emit_att(g, j, c, hp):
            par = g % 2
            slot = hp % 2
            qk_cols = slice(2048 * par + 256 * hp + 128 * j,
                            2048 * par + 256 * hp + 128 * (j + 1))
            if hp > 0:
                pc, php, pj = c, hp - 1, j
            elif c > 0:
                pc, php, pj = c - 1, 7, (c - 1) % G
            else:
                pc = None
            if hp >= 2:
                p2c, p2hp = c, hp - 2
            elif c > 0:
                p2c, p2hp = c - 1, hp + 6
            else:
                p2c = None
            r.wait("tensor", sVE, E[("qkstt", g, hp // 2)])
            r.wait("tensor", sVE, E[("qkstt", g, 4 + hp // 2)])
            prev = E.get(("atlo", p2c, p2hp)) if p2c is not None else None
            if prev:
                r.wait("tensor", sVE, prev)
            rg(0, 64)
            E[("st1", c, hp)] = r.op("tensor", "matmul",
                                     pSTg[0][:, 128 * slot:128 * (slot + 1)],
                                     kfm[0:64, qk_cols], qfm[0:64, qk_cols],
                                     start=True, stop=True, inc=sTN)
            pn = E.get(("ndmm", c, hp - 2))
            if pn is None and c > 0:
                pn = E.get(("ndmm", c - 1, hp + 6))
            r.wait("vector", sTN, E[("st1", c, hp)])
            if pn:
                r.wait("vector", sTN, pn)
            E[("atlo", c, hp)] = r.op(
                "vector", "tensor_tensor", at_sb[:, 256 * slot:256 * slot + 128],
                pSTg[0][:, 128 * slot:128 * (slot + 1)], maskT[:, 0:128],
                OP.mult, inc=sVE)
            kprev = E.get(("ktok", pc, php)) if pc is not None else None
            if kprev:
                r.wait("tensor", sSC, kprev)
            rg(0, 128)
            E[("kt", c, hp)] = r.op("tensor", "transpose", pKT[:, :],
                                    kfm[:, qk_cols], idb[:, :], inc=sTN)
            prevh = E.get(("at", p2c, p2hp)) if p2c is not None else None
            if prevh:
                r.wait("tensor", sVE, prevh)
            rg(64, 64)
            E[("st2", c, hp)] = r.op("tensor", "matmul",
                                     pSTg[1][:, 128 * slot:128 * (slot + 1)],
                                     kfm[64:128, qk_cols], qfm[64:128, qk_cols],
                                     start=True, stop=True, inc=sTN)
            r.wait("vector", sTN, E[("st2", c, hp)])
            if pn:
                r.wait("vector", sTN, pn)
            E[("at", c, hp)] = r.op(
                "vector", "tensor_tensor",
                at_sb[:, 256 * slot + 128:256 * (slot + 1)],
                pSTg[1][:, 128 * slot:128 * (slot + 1)], maskT[:, 128:256],
                OP.mult, inc=sVE)
            if pc is not None:
                emit_ds(pc, php, pj)
            else:
                rg(0, 128)
                r.op("tensor", "matmul", pDS[0:64, 0:1], idb[:, 0:64],
                     idb[:, 0:1], start=True, stop=True)
            sb_prev = E.get(("sbf", c - 1, hp))
            if sb_prev:
                r.wait("tensor", sSC, sb_prev)
            elif c == 0:
                r.wait("tensor", sGP, E[("const",)])
            yprev = E.get(("ndcp", pc, php)) if pc is not None else None
            if yprev:
                r.wait("tensor", sSC, yprev)
            r.wait("tensor", sSC, E[("vevS", c, 0 if hp < 4 else 1)])
            rg(0, 64)
            r.op("tensor", "matmul", pND[:, 0:65],
                 qfm[0:64, qk_cols], saugbf[0:64, 65 * hp:65 * (hp + 1)],
                 start=True, stop=False)
            r.wait("tensor", sVE, E[("atlo", c, hp)])
            rg(0, 128)
            r.op("tensor", "matmul", pND[:, 0:65],
                 at_sb[:, 256 * slot:256 * slot + 128],
                 vaug[:, 1040 * j + 65 * 2 * hp:1040 * j + 65 * (2 * hp + 1)],
                 start=False, stop=False)
            rg(64, 64)
            r.op("tensor", "matmul", pND[:, 65:130],
                 qfm[64:128, qk_cols], saugbf[64:128, 65 * hp:65 * (hp + 1)],
                 start=False, stop=False)
            r.wait("tensor", sVE, E[("at", c, hp)])
            rg(0, 128)
            E[("ndmm", c, hp)] = r.op(
                "tensor", "matmul", pND[:, 65:130],
                at_sb[:, 256 * slot + 128:256 * (slot + 1)],
                vaug[:, 1040 * j + 65 * (2 * hp + 1):1040 * j + 65 * (2 * hp + 2)],
                start=False, stop=True, inc=sTN)
            r.wait("scalar", sTN, E[("kt", c, hp)])
            dsprev = E.get(("dsmm", c - 1 if hp <= 1 else c, hp - 2 if hp > 1 else hp + 6))
            if dsprev:
                r.wait("scalar", sTN, dsprev)
            E[("ktok", c, hp)] = r.op(
                "scalar", "copy", ktok[:, 128 * slot:128 * (slot + 1)], pKT[:, :], inc=sSC)
            r.wait("scalar", sTN, E[("ndmm", c, hp)])
            nds = 130 * (hp % 2)
            E[("ndcp", c, hp)] = r.op("scalar", "copy",
                                      ndbuf[:, nds:nds + 130], pND[:, :], inc=sSC)

            def norm():
                r.wait("vector", sSC, E[("ndcp", c, hp)])
                r.drain("vector")
                ndr = ndbuf[:, nds:nds + 130].rearrange("p (h c2) -> p h c2", h=2)
                r.op("vector", "tensor_scalar", r2pre[:, :], ndr[:, :, 64],
                     ATT_EPS, None, OP.add)
                r.drain("vector")
                r.op("vector", "reciprocal", r2[:, :], r2pre[:, :])
                r.drain("vector")
                r2ap = r2[:, :]
                r2e = bass.AP(r2ap.tensor, r2ap.offset, [list(r2ap.ap[0]), [1, 2], [0, 64]])
                if c >= G and hp == 0:
                    r.wait("vector", sTN, E[("ytr", c - G)])
                E[("yev", c, hp)] = r.op(
                    "vector", "tensor_tensor",
                    ytok[:, 1024 * j + 128 * hp:1024 * j + 128 * (hp + 1)],
                    ndr[:, :, 0:64], r2e, OP.mult, inc=sVE)
            defer_post(1, norm)

        emit_lns(0, spread=False)
        for jj in range(G):
            emit_xtr(0, jj, inline=True)
        for pp in range(8):
            emit_qk(0, pp, inline=True)
        emit_v(0, 0, 0, inline=True)
        emit_v(0, 0, 1, inline=True)

        LN2APPLY = {}
        xn2ref = [None]
        for g in range(NG):
            c0, c1 = G * g, G * g + 1
            nxt = g + 1 < NG
            pc0, pc1 = c0 - 2, c1 - 2
            sched = {i: [] for i in range(16)}
            if nxt:
                sched[0].append(lambda g=g: emit_lns(g + 1, spread=True))
            sched[0].append(lambda g=g: emit_v(g, 1, 0))
            sched[1].append(lambda g=g: emit_v(g, 1, 1))
            if g > 0:
                sched[0].append(lambda g=g, pc0=pc0: emit_ytr(g - 1, 0, pc0))
                sched[2].append(lambda g=g, pc0=pc0: emit_proj(g - 1, 0, pc0, 0))
                sched[2].append(lambda g=g, pc0=pc0: emit_proj(g - 1, 0, pc0, 1))
                sched[3].append(lambda g=g, pc1=pc1: emit_ytr(g - 1, 1, pc1))
                sched[4].append(lambda g=g, pc1=pc1: emit_proj(g - 1, 1, pc1, 0))
                sched[4].append(lambda g=g, pc1=pc1: emit_proj(g - 1, 1, pc1, 1))
            if nxt:
                sched[4].append(lambda g=g: emit_xtr(g + 1, 0))
                sched[6].append(lambda g=g: emit_xtr(g + 1, 1))
                for i in range(8):
                    sched[7 + i].append(lambda g=g, p=i: emit_qk(g + 1, p))
                sched[15].append(lambda g=g: emit_v(g + 1, 0, 0))
            if g > 0:
                for ci in range(2):
                    pc = 2 * (g - 1) + ci
                    pslot = pc % 2
                    th = ln(pc, statsB,
                            (lambda pslot=pslot: xn2ref[0][:, 1024 * pslot:1024 * (pslot + 1)]),
                            vec_apply=True)
                    LN2APPLY[pc] = th[-1]

                    def wrap0(pc=pc, t0=th[0]):
                        r.wait("scalar", sVE, E[("res1", pc)])
                        t0()
                    base = 8 + 4 * ci

                    def pack(fns):
                        def f():
                            for fn in fns:
                                fn()
                        return f
                    sched[base].append(wrap0)
                    sched[base + 1].append(th[1])
                    sched[base + 2].append(pack(th[2:4]))
                    sched[base + 3].append(pack(th[4:8]))
            for blk in range(16):
                j = blk // 8
                c = c0 if blk < 8 else c1
                hp = blk % 8
                flush()
                if blk == 0 and g > 0:
                    emit_v(g, 0, 1)
                emit_att(g, j, c, hp)
                flush_post()
                for fn in sched[blk]:
                    fn()
                BI["n"] += 1
            cut(f"A{g}")
        flush_all()
        emit_ds(TC - 1, NHP - 1, 1)
        flush_all()
        emit_ytr(NG - 1, 0, TC - 2, inline=True)
        emit_proj(NG - 1, 0, TC - 2, 0, inline=True)
        emit_proj(NG - 1, 0, TC - 2, 1, inline=True)
        emit_ytr(NG - 1, 1, TC - 1, inline=True)
        emit_proj(NG - 1, 1, TC - 1, 0, inline=True)
        emit_proj(NG - 1, 1, TC - 1, 1, inline=True)

        for pc in (TC - 2, TC - 1):
            th = ln(pc, statsB,
                    (lambda pslot=pc % 2: xn2ref[0][:, 1024 * pslot:1024 * (pslot + 1)]),
                    vec_apply=True)
            LN2APPLY[pc] = th[-1]
            r.wait("scalar", sVE, E[("res1", pc)])
            for fn in th[:-1]:
                fn()

        E[("phaseA_tn",)] = sTN.n
        E[("phaseA_ve",)] = sVE.n
        E[("phaseA_sc",)] = sSC.n
        E[("phaseA_gp",)] = sGP.n

      with ExitStack() as bctx:
          bsb = lambda name, shape, dt: bctx.enter_context(nc.sbuf_tensor(name, shape, dt))
          bps = lambda name, shape, dt: bctx.enter_context(nc.psum_tensor(name, shape, dt))

          xn2fm = bsb("xn2fm", [128, KC * 2048], BF16)
          acc = bsb("acc", [128, 4 * 2048], F32)
          ewt = bsb("ewt", [128, 2 * 4096], BF16)
          ewn1 = bsb("ewn", [128, 4096], BF16)
          ewnb = [ewn0, ewn1]
          xn2tok = bsb("xn2tok", [128, 2 * 1024], BF16)
          xn2ref[0] = xn2tok
          gbuf = bsb("gbuf", [128, 16 * 512], BF16)

          pT2 = [bps("pT2a", [128, 512], BF16), bps("pT2b", [128, 512], BF16)]
          pE = [bps("pE0", [128, 512], F32), bps("pE1", [128, 512], F32)]
          gctx = ExitStack()
          gps = lambda name, shape, dt: gctx.enter_context(nc.psum_tensor(name, shape, dt))
          pPL = [gps("pPL0", [1, 512], F32), gps("pPL1", [1, 512], F32)]
          pG = gps("pG", [128, 8], F32)
          pFs = [None, None]

          for eng in ("tensor", "vector", "scalar", "gpsimd"):
              r.wait(eng, sTN, E[("phaseA_tn",)])
              r.wait(eng, sVE, E[("phaseA_ve",)])
              r.wait(eng, sSC, E[("phaseA_sc",)])
              r.wait(eng, sGP, E[("phaseA_gp",)])

          def emit_ew_load(oh, e, idx):
              buf = idx % 2
              if idx >= 2:
                  r.wait("gpsimd", sTN, E[("ewtr", idx - 2)])
              src = ew_d[e, 512 * oh:512 * (oh + 1), :].rearrange("(oc p) d -> p oc d", p=128)
              dst = ewnb[buf][:, :].rearrange("p (oc d) -> p oc d", d=1024)
              E[("ewld", idx)] = r.op("gpsimd", "dma_start", inc=sEW[buf], k=16, out=dst, in_=src)

          def emit_ewtr_group(idx, oc4):
              buf = idx % 2
              if oc4 == 0:
                  r.wait("tensor", sEW[buf], E[("ewld", idx)])
              for b in range(2):
                  prev = E.get(("pt2free", b))
                  if prev:
                      r.wait("tensor", sVE, prev)
                  for i in range(4):
                      kc = 4 * b + i
                      rg(0, 128)
                      r.op("tensor", "transpose",
                           pT2[b][:, 128 * i:128 * (i + 1)],
                           ewnb[buf][:, 1024 * oc4 + 128 * kc:
                                     1024 * oc4 + 128 * (kc + 1)],
                           idb[:, :], inc=(sTN if i == 3 else None))
                  E[("ewtrb", idx, oc4, b)] = sTN.n
                  r.wait("vector", sTN, E[("ewtrb", idx, oc4, b)])
                  ew = ewt[:, :]
                  dst = bass.AP(ew.tensor,
                                ew.offset + 4096 * buf + 512 * (4 * b) + 128 * oc4,
                                [list(ew.ap[0]), [512, 4], [1, 128]])
                  E[("pt2free", b)] = r.op("vector", "tensor_copy", dst,
                                           pT2[b][:, :], inc=sVE)
              E[("ewtevq", idx, oc4)] = sVE.n
              if oc4 == 3:
                  E[("ewtr", idx)] = sTN.n
                  E[("ewtev", idx)] = sVE.n

          def emit_ewtr(idx):
              for oc4 in range(4):
                  emit_ewtr_group(idx, oc4)

          TID = {"n": 0}
          PSTT = []

          def emit_etile(idx, oh, e, ts, oc, defer_stt):
              buf = idx % 2
              tid = TID["n"]
              TID["n"] += 1
              pb = pE[tid % 2]
              prevg = E.get(("geluk", tid - 2))
              if prevg is not None:
                  r.wait("tensor", sSC, prevg)
              r.wait("tensor", sVE, E[("ewtevq", idx, oc)])
              r.wait("tensor", sVE, E[("x2e", 4 * ts + 3)])
              for kc in range(KC):
                  rg(0, 128)
                  r.op("tensor", "matmul", pb[:, :],
                       ewt[:, 4096 * buf + 512 * kc + 128 * oc:
                           4096 * buf + 512 * kc + 128 * (oc + 1)],
                       xn2fm[:, 2048 * kc + 512 * ts:2048 * kc + 512 * (ts + 1)],
                       start=(kc == 0), stop=(kc == KC - 1),
                       inc=(sTN if kc == KC - 1 else None))
              e_mm = sTN.n
              r.wait("scalar", sTN, e_mm)
              gslot = tid % 16
              prevs = E.get(("sttk", tid - 16))
              if prevs is not None:
                  r.wait("scalar", sVE, prevs)
              if tid == 0:
                  r.wait("scalar", sVE, E[("ebev", KC - 1)])
              E[("geluk", tid)] = r.op(
                  "scalar", "activation", gbuf[:, 512 * gslot:512 * (gslot + 1)],
                  pb[:, :], AF.Gelu,
                  bias=expbfm[:, 8 * (4 * oh + oc) + e:8 * (4 * oh + oc) + e + 1],
                  inc=sSC)

              def stt():
                  r.wait("vector", sSC, E[("geluk", tid)])
                  if tid == 0:
                      r.wait("vector", sVE, E[("gates",)])
                  r.drain("vector")
                  if idx == 8:
                      r.wait("vector", sTN, E[("ftr", 0, 4 * ts + 3)])
                  aslice = acc[:, 2048 * oc + 512 * ts:2048 * oc + 512 * (ts + 1)]
                  if e == 0:
                      E[("sttk", tid)] = r.op(
                          "vector", "tensor_scalar", aslice,
                          gbuf[:, 512 * gslot:512 * (gslot + 1)],
                          gatesb[:, e:e + 1], None, OP.mult, inc=sVE)
                  else:
                      E[("sttk", tid)] = r.op(
                          "vector", "scalar_tensor_tensor", aslice,
                          gbuf[:, 512 * gslot:512 * (gslot + 1)],
                          gatesb[:, e:e + 1], aslice, OP.mult, OP.add, inc=sVE)
                  E[("acc", idx, ts, oc)] = E[("sttk", tid)]
              if defer_stt:
                  PSTT.append(stt)
              else:
                  stt()

          emit_ew_load(0, 1, 1)

          r.wait("tensor", sGWl, 16)
          for kc in range(KC):
              rg(0, 8)
              e_t = r.op("tensor", "transpose", pG[:, :],
                         gw_s[0:8, 128 * kc:128 * (kc + 1)], idf[0:8, 0:8], inc=sTN)
              r.wait("vector", sTN, e_t)
              E[("gwev", kc)] = r.op("vector", "tensor_copy", gwT[:, 8 * kc:8 * (kc + 1)],
                                     pG[:, :], inc=sVE)
              r.wait("tensor", sVE, E[("gwev", kc)])
          r.wait("tensor", sEBl, 16)
          for oc in range(KC):
              rg(0, 8)
              e_t = r.op("tensor", "transpose", pG[:, :],
                         eb_s[0:8, 128 * oc:128 * (oc + 1)], idf[0:8, 0:8], inc=sTN)
              r.wait("vector", sTN, e_t)
              E[("ebev", oc)] = r.op("vector", "tensor_copy", expbfm[:, 8 * oc:8 * (oc + 1)],
                                     pG[:, :], inc=sVE)
              r.wait("tensor", sVE, E[("ebev", oc)])

          for c in range(TC):
              slot = c % 2
              if c >= 2:
                  r.wait("vector", sTN, E[("xtr2", c - 2)])
              if c in LN2APPLY:
                  LN2APPLY[c]()
              else:
                  for _t in ln(c, statsB, xn2tok[:, 1024 * slot:1024 * (slot + 1)],
                               vec_apply=True):
                      _t()
              E[("xn2", c)] = E[("xn1", c)]
              r.wait("tensor", sVE, E[("xn2", c)])
              for b in range(2):
                  prev = E.get(("pt2free", b))
                  if prev:
                      r.wait("tensor", sVE, prev)
                  for i in range(4):
                      kc = 4 * b + i
                      rg(0, 128)
                      r.op("tensor", "transpose", pT2[b][:, 128 * i:128 * (i + 1)],
                           xn2tok[:, 1024 * slot + 128 * kc:1024 * slot + 128 * (kc + 1)],
                           idb[:, :], inc=(sTN if i == 3 else None))
                  E[("xtr2b", c, b)] = sTN.n
                  r.wait("vector", sTN, E[("xtr2b", c, b)])
                  xf2 = xn2fm[:, :]
                  dst = bass.AP(xf2.tensor, xf2.offset + 2048 * (4 * b) + 128 * c,
                                [list(xf2.ap[0]), [2048, 4], [1, 128]])
                  E[("pt2free", b)] = r.op("vector", "tensor_copy", dst, pT2[b][:, :], inc=sVE)
              E[("xtr2", c)] = sTN.n
              E[("x2e", c)] = sVE.n
              s0 = 8 * c
              for half in range(2):
                  rg(0, 128)
                  r.op("tensor", "matmul", pPL[half][0:1, :],
                       statsB[:, s0 + 5:s0 + 6],
                       xres[:, 1024 * c + 512 * half:1024 * c + 512 * (half + 1)],
                       start=(c == 0), stop=(c == TC - 1),
                       inc=(sTN if c == TC - 1 else None))
              if c == 4:
                  emit_ewtr(0)
              elif c == 5:
                  emit_ewtr(1)
              elif c in (6, 7):
                  for _oc in (0, 1) if c == 6 else (2, 3):
                      emit_etile(0, 0, 0, 0, _oc, True)
              elif c in (8, 9):
                  for _oc in (0, 1) if c == 8 else (2, 3):
                      emit_etile(0, 0, 0, 1, _oc, True)
              elif c in (12, 13):
                  for _oc in (0, 1) if c == 12 else (2, 3):
                      emit_etile(0, 0, 0, 2, _oc, True)
          E[("plmm",)] = sTN.n
          E[("xn2fm_done",)] = sVE.n
          cut("Bpro")

          stb = statsB[:, :]
          bias_cols = bass.AP(stb.tensor, stb.offset + 6, [list(stb.ap[0]), [8, TC]])
          r.drain("vector")
          e_rs = r.op("vector", "reduce_sum", pooledfm[:, 0:1], bias_cols, axis=AX, inc=sVE)
          r.wait("tensor", sVE, e_rs)
          rg(0, 128)
          e_bs = r.op("tensor", "matmul", pG[0:1, 0:1], pooledfm[:, 0:1], onesf[:, 0:1],
                      start=True, stop=True, inc=sTN)
          emit_etile(0, 0, 0, 3, 0, True)
          r.wait("vector", sTN, e_bs)
          e_bsv = r.op("vector", "tensor_copy", small[0:1, 1:2], pG[0:1, 0:1], inc=sVE)
          r.wait("vector", sTN, E[("plmm",)])
          r.drain("vector")
          for half in range(2):
              r.op("vector", "tensor_scalar", pooled_sb[0:1, 512 * half:512 * (half + 1)],
                   pPL[half][0:1, :], small[0:1, 1:2], 1.0 / 2048, OP.add, OP.mult)
          r.drain("vector")
          e_pool = r.op("vector", "tensor_copy", small[0:1, 2:3], pooled_sb[0:1, 0:1], inc=sVE)
          r.wait("tensor", sVE, e_pool)
          for kc in range(KC):
              rg(0, 1)
              r.op("tensor", "transpose", pG[:, kc:kc + 1],
                   pooled_sb[0:1, 128 * kc:128 * (kc + 1)], idf[0:1, 0:1],
                   inc=(sTN if kc == KC - 1 else None))
          E[("plfm",)] = sTN.n
          emit_etile(0, 0, 0, 3, 1, True)
          r.wait("vector", sTN, E[("plfm",)])
          e_plfm = r.op("vector", "tensor_copy", pooledfm[:, :], pG[:, :], inc=sVE)
          r.wait("tensor", sVE, e_plfm)
          for kc in range(KC):
              rg(0, 128)
              r.op("tensor", "matmul", pPL[0][0:1, 0:8],
                   pooledfm[:, kc:kc + 1], gwT[:, 8 * kc:8 * (kc + 1)],
                   start=(kc == 0), stop=(kc == KC - 1),
                   inc=(sTN if kc == KC - 1 else None))
          E[("lgmm",)] = sTN.n
          emit_etile(0, 0, 0, 3, 2, True)
          r.wait("vector", sTN, E[("lgmm",)])
          r.wait("vector", sGBl, 16)
          r.drain("vector")
          r.op("vector", "tensor_add", lg_sb[0:1, :], pPL[0][0:1, 0:8], gb_s[0:1, :])
          r.drain("vector")
          r.op("vector", "reduce_max", small[0:1, 4:5], lg_sb[0:1, :], axis=AX, negate=True)
          r.drain("vector")
          r.op("vector", "tensor_scalar", small[0:1, 5:6], small[0:1, 4:5], -1.0, None, OP.mult)
          r.drain("vector")
          r.op("vector", "tensor_scalar", tt_sb[0:1, 0:8], lg_sb[0:1, :], small[0:1, 5:6],
               None, OP.is_equal)
          r.drain("vector")
          r.op("vector", "scalar_tensor_tensor", tt_sb[0:1, 8:16], tt_sb[0:1, 0:8], 1e30,
               lg_sb[0:1, :], OP.mult, OP.subtract)
          r.drain("vector")
          r.op("vector", "tensor_reduce", small[0:1, 6:7], tt_sb[0:1, 8:16], axis=AX, op=OP.min)
          r.drain("vector")
          e_m2 = r.op("vector", "tensor_scalar", small[0:1, 7:8], small[0:1, 6:7],
                      -1.0, None, OP.mult, inc=sVE)
          r.wait("scalar", sVE, e_m2)
          e_ex = r.op("scalar", "activation", tt_sb[0:1, 8:16], lg_sb[0:1, :], AF.Exp,
                      bias=small[0:1, 4:5], inc=sSC)
          r.wait("vector", sSC, e_ex)
          r.drain("vector")
          r.op("vector", "tensor_scalar", tt_sb[0:1, 0:8], lg_sb[0:1, :], small[0:1, 7:8],
               None, OP.is_ge)
          r.drain("vector")
          r.op("vector", "tensor_mul", tt_sb[0:1, 0:8], tt_sb[0:1, 0:8], tt_sb[0:1, 8:16])
          r.drain("vector")
          r.op("vector", "reduce_sum", small[0:1, 8:9], tt_sb[0:1, 0:8], axis=AX)
          r.drain("vector")
          r.op("vector", "reciprocal", small[0:1, 9:10], small[0:1, 8:9])
          r.drain("vector")
          e_gate = r.op("vector", "tensor_scalar", tt_sb[0:1, 0:8], tt_sb[0:1, 0:8],
                        small[0:1, 9:10], None, OP.mult, inc=sVE)
          r.wait("tensor", sVE, e_gate)
          rg(0, 1)
          e_gb = r.op("tensor", "matmul", pG[:, :], onesf[0:1, :], tt_sb[0:1, 0:8],
                      start=True, stop=True, inc=sTN)
          emit_etile(0, 0, 0, 3, 3, True)
          r.wait("vector", sTN, e_gb)
          E[("gates",)] = r.op("vector", "tensor_copy", gatesb[:, :], pG[:, :], inc=sVE)
          cut("Bgates")

          for fn in PSTT:
              fn()
          PSTT.clear()
          gctx.close()
          pFs[0] = bps("pF", [128, 512], F32)
          pFs[1] = bps("pF2", [128, 512], F32)

          def emit_final_chunk(oh, c):
                  ts, u = c // 4, c % 4
                  fp = (16 * oh + c) % 2
                  pFb = pFs[fp]
                  r.wait("tensor", sVE, E[("acc", 8 * oh + 7, ts, 3)])
                  prevf = E.get(("ffp", fp))
                  if prevf:
                      r.wait("tensor", sVE, prevf)
                  for oc in range(4):
                      rg(0, 128)
                      r.op("tensor", "transpose", pFb[:, 128 * oc:128 * (oc + 1)],
                           acc[:, 2048 * oc + 512 * ts + 128 * u:
                               2048 * oc + 512 * ts + 128 * (u + 1)],
                           idf[:, :], inc=(sTN if oc == 3 else None))
                  E[("ftr", oh, c)] = sTN.n
                  r.wait("vector", sTN, E[("ftr", oh, c)])
                  r.drain("vector")
                  E[("ffp", fp)] = r.op(
                      "vector", "tensor_add",
                      xres[:, 1024 * c + 512 * oh:1024 * c + 512 * (oh + 1)],
                      xres[:, 1024 * c + 512 * oh:1024 * c + 512 * (oh + 1)],
                      pFb[:, :], inc=sVE)
                  E[("fin", oh, c)] = sVE.n

          def emit_store(oh):
              for half in range(2):
                  for c in range(8 * half, 8 * (half + 1)):
                      r.wait("sync", sVE, E[("fin", oh, c)])
                  r.op(
                      "sync", "dma_start", inc=sST, k=16,
                      out=out_d[1024 * half:1024 * (half + 1),
                                512 * oh:512 * (oh + 1)].rearrange("(c p) d -> p c d", p=128),
                      in_=bass.AP(xres[:, :].tensor,
                                  xres[:, :].offset + 8192 * half + 512 * oh,
                                  [list(xres[:, :].ap[0]), [1024, 8], [1, 512]]))

          emit_ew_load(0, 2, 2)
          for idx in range(1, 16):
              oh, e = idx // 8, idx % 8
              if idx + 2 < 16:
                  n_idx = idx + 2
                  emit_ew_load(n_idx // 8, n_idx % 8, n_idx)
              for ts in range(4):
                  if idx == 8:
                      for u in range(4):
                          emit_final_chunk(0, 4 * ts + u)
                  for oc in range(4):
                      emit_etile(idx, oh, e, ts, oc, False)
                  if 2 <= idx + 1 < 16:
                      emit_ewtr_group(idx + 1, ts)
                  if idx == 15:
                      for u in range(4):
                          emit_final_chunk(1, 4 * ts + u)
              if idx == 8:
                  emit_store(0)
          emit_store(1)

          r.wait("sync", sST, sST.n)

    except _Cut:
        r.op("sync", "dma_start", inc=sST, k=16,
             out=out_d[:, :].rearrange("(c p) d -> p c d", p=128),
             in_=xres[:, :].rearrange("p (c d) -> p c d", d=1024))
        r.wait("sync", sST, sST.n)
        _replay(nc, r)
        return nc
    _replay(nc, r)
    ctx.close()
    return nc


def _replay(nc, r):
    with nc.Block() as block:
        def make_body(eng_name):
            def body(eng):
                for meth, args, kw, inc, k in r.q[eng_name]:
                    res = getattr(eng, meth)(*args, **kw)
                    if inc is not None:
                        res.then_inc(inc.h, k)
            return body
        block.tensor(make_body("tensor"))
        block.vector(make_body("vector"))
        block.scalar(make_body("scalar"))
        block.gpsimd(make_body("gpsimd"))
        block.sync(make_body("sync"))



_NC_CACHE = {}


def kernel(L0_attn_x, L0_attn_w_attn, L0_attn_w_proj,
           L0_ffn_gate_w, L0_ffn_gate_b, L0_ffn_expert_w, L0_ffn_expert_b):
    x = np.ascontiguousarray(np.asarray(L0_attn_x, dtype=np.float32))
    wa = np.ascontiguousarray(np.asarray(L0_attn_w_attn, dtype=np.float32))
    wp = np.ascontiguousarray(np.asarray(L0_attn_w_proj, dtype=np.float32))
    gw = np.ascontiguousarray(np.asarray(L0_ffn_gate_w, dtype=np.float32))
    gb = np.ascontiguousarray(np.asarray(L0_ffn_gate_b, dtype=np.float32)).reshape(1, 8)
    ew = np.ascontiguousarray(np.asarray(L0_ffn_expert_w, dtype=np.float32))
    eb = np.ascontiguousarray(np.asarray(L0_ffn_expert_b, dtype=np.float32))

    if "nc" not in _NC_CACHE:
        _NC_CACHE["nc"] = build_program()
    nc = _NC_CACHE["nc"]
    in_maps = [
        {"x": x[b], "w_attn": wa, "w_proj": wp, "gate_w": gw,
         "gate_b": gb, "expert_w": ew, "expert_b": eb}
        for b in range(8)
    ]
    res = run_bass_kernel_spmd(nc, in_maps, core_ids=list(range(8)))
    out = np.stack([res.results[b]["out"] for b in range(8)], axis=0)
    return out.astype(np.float32)


# version 35
# speedup vs baseline: 1.0023x; 1.0023x over previous
import numpy as np
from contextlib import ExitStack

import concourse.bass as bass
import concourse.mybir as mybir
from concourse.bass_utils import run_bass_kernel_spmd

F32 = mybir.dt.float32
BF16 = mybir.dt.bfloat16
AF = mybir.ActivationFunctionType
OP = mybir.AluOpType
AX = mybir.AxisListType.X

T, D = 2048, 1024
H, DH = 16, 64
TC = T // 128
KC = D // 128
NHP = H // 2
G = 2
NG = TC // G
LN_EPS = 1e-5
ATT_EPS = 1e-6

ENGINES = ("tensor", "vector", "scalar", "gpsimd", "sync")


class Sem:
    __slots__ = ("h", "n")

    def __init__(self, h):
        self.h = h
        self.n = 0


class Rec:
    def __init__(self):
        self.q = {e: [] for e in ENGINES}

    def wait(self, eng, sem, val):
        if val is not None and val > 0:
            self.q[eng].append(("wait_ge", (sem.h, val), {}, None, 0))

    def op(self, eng, meth, *args, inc=None, k=1, **kw):
        if inc is not None:
            inc.n += k
        self.q[eng].append((meth, args, kw, inc, k))
        return inc.n if inc is not None else None

    def drain(self, eng):
        self.q[eng].append(("drain", (), {}, None, 0))


class _Cut(Exception):
    pass


def build_program(upto=None):
    def cut(label):
        if upto is not None and label == upto:
            raise _Cut()
    nc = bass.Bass("TRN2", target_bir_lowering=False)
    x_d = nc.dram_tensor("x", [T, D], F32, kind="ExternalInput")
    wa_d = nc.dram_tensor("w_attn", [D, 3 * D], F32, kind="ExternalInput")
    wp_d = nc.dram_tensor("w_proj", [D, D], F32, kind="ExternalInput")
    gw_d = nc.dram_tensor("gate_w", [8, D], F32, kind="ExternalInput")
    gb_d = nc.dram_tensor("gate_b", [1, 8], F32, kind="ExternalInput")
    ew_d = nc.dram_tensor("expert_w", [8, D, D], F32, kind="ExternalInput")
    eb_d = nc.dram_tensor("expert_b", [8, D], F32, kind="ExternalInput")
    out_d = nc.dram_tensor("out", [T, D], F32, kind="ExternalOutput")

    ctx = ExitStack()
    sb = lambda name, shape, dt: ctx.enter_context(nc.sbuf_tensor(name, shape, dt))
    ps = lambda name, shape, dt: ctx.enter_context(nc.psum_tensor(name, shape, dt))
    sem = lambda name: Sem(ctx.enter_context(nc.semaphore(name)))

    r = Rec()
    E = {}
    _rg = {"base": 0, "rows": 128}

    def wait_ev(eng, ev):
        if ev is not None:
            s, v = ev
            r.wait(eng, s, v)

    def rg(base, rows):
        if rows < 128 and _rg["rows"] < 128 and (_rg["base"], _rg["rows"]) != (base, rows):
            raise AssertionError(f"row-group hazard: {_rg} -> {(base, rows)}")
        _rg["base"], _rg["rows"] = base, rows

    sX0 = sem("sX0")
    sX0p = [sem("sX0a"), sem("sX0b")]
    sX1 = sem("sX1")
    sGWl = sem("sGWl")
    sGBl = sem("sGBl")
    sEBl = sem("sEBl")
    sWA = sem("sWA")
    sWAp = [sem(f"sWAp{i}") for i in range(4)]
    sWP = sem("sWP")
    sEW = [sem("sEW0"), sem("sEW1")]
    sTN = sem("sTN")
    sVE = sem("sVE")
    sSC = sem("sSC")
    sGP = sem("sGP")
    sST = sem("sST")

    xres = sb("xres", [128, TC * 1024], F32)
    idb = sb("idb", [128, 128], BF16)
    idf = sb("idf", [128, 128], F32)
    maskT = sb("maskT", [128, 256], F32)
    onesf = sb("onesf", [128, 128], F32)
    saug = sb("saug", [128, NHP * 65], F32)
    saugbf = sb("saugbf", [128, NHP * 65], BF16)
    statsA = sb("statsA", [128, TC * 8], F32)
    statsB = sb("statsB", [128, TC * 8], F32)
    gw_s = sb("gw_s", [8, 1024], F32)
    gb_s = sb("gb_s", [1, 8], F32)
    eb_s = sb("eb_s", [8, 1024], F32)
    gwT = sb("gwT", [128, 64], F32)
    expbfm = sb("expbfm", [128, 64], F32)
    pooledfm = sb("pooledfm", [128, 8], F32)
    pooled_sb = sb("pooled_sb", [1, 1024], F32)
    gatesb = sb("gatesb", [128, 8], F32)
    small = sb("small", [1, 32], F32)
    lg_sb = sb("lg_sb", [1, 8], F32)
    tt_sb = sb("tt_sb", [1, 16], F32)
    sqbuf = sb("sqbuf", [128, 1024], F32)
    r2pre = sb("r2pre", [128, 2], F32)
    r2 = sb("r2", [128, 2], F32)
    epscol = sb("epscol", [128, 1], F32)

    for q in range(2):
        srcq = x_d[512 * q:512 * (q + 1), :].rearrange("(c p) d -> p c d", p=128)
        dstq = xres[:, 4096 * q:4096 * (q + 1)].rearrange("p (c d) -> p c d", d=1024)
        r.op("sync", "dma_start", inc=sX0p[q], k=16, out=dstq, in_=srcq)
    src1 = x_d[1024:2048, :].rearrange("(c p) d -> p c d", p=128)
    dst1 = xres[:, 8192:16384].rearrange("p (c d) -> p c d", d=1024)
    r.op("sync", "dma_start", inc=sX1, k=16, out=dst1, in_=src1)
    r.op("sync", "dma_start", inc=sGWl, k=16, out=gw_s[:, :], in_=gw_d[:, :])
    r.op("sync", "dma_start", inc=sGBl, k=16, out=gb_s[:, :], in_=gb_d[:, :])
    r.op("sync", "dma_start", inc=sEBl, k=16, out=eb_s[:, :], in_=eb_d[:, :])

    r.op("gpsimd", "memset", idb[:, :], 0.0)
    r.drain("gpsimd")
    r.op("gpsimd", "affine_select", out=idb[:, :], in_=idb[:, :], compare_op=OP.not_equal,
         fill=1.0, base=0, pattern=[[-1, 128]], channel_multiplier=1)
    r.op("gpsimd", "memset", idf[:, :], 0.0)
    r.drain("gpsimd")
    r.op("gpsimd", "affine_select", out=idf[:, :], in_=idf[:, :], compare_op=OP.not_equal,
         fill=1.0, base=0, pattern=[[-1, 128]], channel_multiplier=1)
    r.op("gpsimd", "memset", maskT[:, :], 0.0)
    r.drain("gpsimd")
    for h in range(2):
        r.op("gpsimd", "affine_select", out=maskT[:, 128 * h:128 * (h + 1)],
             in_=maskT[:, 128 * h:128 * (h + 1)], compare_op=OP.is_gt, fill=1.0,
             base=0, pattern=[[-1, 128]], channel_multiplier=1)
    r.op("gpsimd", "memset", onesf[:, :], 1.0)
    r.op("gpsimd", "memset", epscol[:, :], LN_EPS)
    r.op("gpsimd", "memset", saug[:, :], 0.0)
    E[("const",)] = r.op("gpsimd", "memset", saugbf[:, :], 0.0, inc=sGP)

    try:
      with ExitStack() as actx:
        asb = lambda name, shape, dt: actx.enter_context(nc.sbuf_tensor(name, shape, dt))
        aps = lambda name, shape, dt: actx.enter_context(nc.psum_tensor(name, shape, dt))

        wattn = asb("wattn", [128, KC * 3072], BF16)
        wproj = asb("wproj", [128, KC * 1024], BF16)
        xn1 = asb("xn1", [128, G * 1024], BF16)
        xfm = asb("xfm", [128, 2 * KC * 128 * G], BF16)
        qfm = asb("qfm", [128, 2 * 8 * 128 * G], BF16)
        kfm = asb("kfm", [128, 2 * 8 * 128 * G], BF16)
        vaug = asb("vaug", [128, G * 1040], BF16)
        ytok = asb("ytok", [128, G * 1024], BF16)
        yfm = asb("yfm", [128, KC * 128 * G], BF16)
        elumin = asb("elumin", [128, 512], F32)
        eluex = asb("eluex", [128, 512], F32)
        at_sb = asb("at_sb", [128, 2 * 256], BF16)
        ktok = asb("ktok", [128, 2 * 128], BF16)

        pbulk = [aps("pbulk0", [128, 512], F32), aps("pbulk1", [128, 512], F32)]
        pT = aps("pT", [128, 1024], BF16)
        pSTg = [aps("pSTa", [128, 256], F32), aps("pSTb", [128, 256], F32)]
        pND = aps("pND", [128, 130], F32)
        pKT = aps("pKT", [128, 128], BF16)
        pDS = aps("pDS", [128, 65], F32)
        ndbuf = asb("ndbuf", [128, 2 * 130], F32)

        for q4 in range(4):
            r.op("gpsimd", "dma_start", inc=sWAp[q4], k=16,
                 out=wattn[:, 3072 * 2 * q4:3072 * 2 * (q4 + 1)]
                 .rearrange("p (kc o) -> p kc o", o=3072),
                 in_=wa_d[256 * q4:256 * (q4 + 1), :]
                 .rearrange("(kc p) o -> p kc o", p=128))
        E[("wproj",)] = r.op(
            "gpsimd", "dma_start", inc=sWP, k=16,
            out=wproj[:, :].rearrange("p (kc o) -> p kc o", o=1024),
            in_=wp_d[:, :].rearrange("(kc p) o -> p kc o", p=128))
        r.wait("gpsimd", sGP, E[("const",)])
        va = vaug[:, :]
        ones_ap = bass.AP(va.tensor, va.offset + 64, [list(va.ap[0]), [1040, G], [65, 16]])
        E[("vones",)] = r.op("gpsimd", "memset", ones_ap, 1.0, inc=sGP)

        def ln(c, stats, xn_dst, vec_apply=False):
            s0 = 8 * c
            xc = xres[:, 1024 * c:1024 * (c + 1)]
            t = []
            dstf = xn_dst if callable(xn_dst) else (lambda: xn_dst)

            def t0():
                r.wait("scalar", sGP, E[("const",)])
                r.drain("scalar")
                r.op("scalar", "activation", sqbuf[:, :], xc, AF.Square,
                     accum_out=stats[:, s0 + 1:s0 + 2])
            t.append(t0)

            def t1():
                r.drain("scalar")
                r.op("scalar", "activation", sqbuf[:, :], xc, AF.Identity,
                     accum_out=stats[:, s0:s0 + 1])
            t.append(t1)

            def t2():
                r.drain("scalar")
                r.op("scalar", "activation", stats[:, s0 + 2:s0 + 3],
                     stats[:, s0:s0 + 1], AF.Identity, scale=-1.0 / 1024)
                r.op("scalar", "activation", stats[:, s0 + 3:s0 + 4],
                     stats[:, s0 + 1:s0 + 2], AF.Identity, scale=1.0 / 1024,
                     bias=epscol[:, :])
            t.append(t2)

            def t3():
                r.drain("scalar")
                r.op("scalar", "activation", stats[:, s0 + 4:s0 + 5],
                     stats[:, s0 + 2:s0 + 3], AF.Square)
            t.append(t3)

            def t4():
                r.drain("scalar")
                r.op("scalar", "activation", stats[:, s0 + 7:s0 + 8],
                     stats[:, s0 + 4:s0 + 5], AF.Identity, scale=-1.0,
                     bias=stats[:, s0 + 3:s0 + 4])
            t.append(t4)

            def t5():
                r.drain("scalar")
                r.op("scalar", "activation", stats[:, s0 + 5:s0 + 6],
                     stats[:, s0 + 7:s0 + 8], AF.Ln)
            t.append(t5)

            def t6():
                r.drain("scalar")
                r.op("scalar", "activation", stats[:, s0 + 5:s0 + 6],
                     stats[:, s0 + 5:s0 + 6], AF.Exp, scale=-0.5)
            t.append(t6)

            def t7():
                r.drain("scalar")
                ev = r.op("scalar", "activation", stats[:, s0 + 6:s0 + 7],
                          stats[:, s0 + 2:s0 + 3], AF.Identity,
                          scale=stats[:, s0 + 5:s0 + 6],
                          inc=(sSC if vec_apply else None))
                if vec_apply:
                    E[("lnst", c)] = ev
            t.append(t7)

            def t8():
                if vec_apply:
                    r.wait("vector", sSC, E[("lnst", c)])
                    r.drain("vector")
                    E[("xn1", c)] = r.op("vector", "tensor_scalar", dstf(), xc,
                                         stats[:, s0 + 5:s0 + 6],
                                         stats[:, s0 + 6:s0 + 7],
                                         OP.mult, OP.add, inc=sVE)
                else:
                    r.drain("scalar")
                    E[("xn1", c)] = r.op("scalar", "activation", dstf(), xc, AF.Identity,
                                         bias=stats[:, s0 + 6:s0 + 7],
                                         scale=stats[:, s0 + 5:s0 + 6], inc=sSC)
            t.append(t8)
            return t

        def emit_ds(pc, php, pj):
            r.wait("tensor", sSC, E[("ktok", pc, php)])
            saprev = E.get(("sadd", pc - 1 if php == 0 else pc, php - 1 if php > 0 else 7))
            if saprev:
                r.wait("tensor", sVE, saprev)
            pslot = php % 2
            rg(0, 128)
            r.op("tensor", "matmul", pDS[0:64, :],
                 ktok[:, 128 * pslot:128 * pslot + 64],
                 vaug[:, 1040 * pj + 65 * 2 * php:1040 * pj + 65 * (2 * php + 1)],
                 start=True, stop=True, tile_position=(0, 0))
            E[("dsmm", pc, php)] = r.op(
                "tensor", "matmul", pDS[64:128, :],
                ktok[:, 128 * pslot + 64:128 * (pslot + 1)],
                vaug[:, 1040 * pj + 65 * (2 * php + 1):1040 * pj + 65 * (2 * php + 2)],
                start=True, stop=True, tile_position=(0, 64), inc=sTN)
            if php == NHP - 1:
                E[("attn_done", pc)] = E[("dsmm", pc, php)]

            def upd():
                r.wait("vector", sTN, E[("dsmm", pc, php)])
                r.drain("vector")
                E[("sadd", pc, php)] = r.op(
                    "vector", "tensor_add", saug[:, 65 * php:65 * (php + 1)],
                    saug[:, 65 * php:65 * (php + 1)], pDS[:, :], inc=sVE)
                r.wait("scalar", sVE, E[("sadd", pc, php)])
                E[("sbf", pc, php)] = r.op(
                    "scalar", "copy", saugbf[:, 65 * php:65 * (php + 1)],
                    saug[:, 65 * php:65 * (php + 1)], inc=sSC)
            defer(1, upd)

        DEF = {}
        BI = {"n": 0}

        DEFP = {}

        def defer(k, fn):
            DEF.setdefault(BI["n"] + k, []).append(fn)

        def defer_post(k, fn):
            DEFP.setdefault(BI["n"] + k, []).append(fn)

        def flush():
            for fn in DEF.pop(BI["n"], []):
                fn()

        def flush_post():
            for fn in DEFP.pop(BI["n"], []):
                fn()

        def flush_all():
            for bi in sorted(set(DEF) | set(DEFP)):
                for fn in DEF.pop(bi, []):
                    fn()
                for fn in DEFP.pop(bi, []):
                    fn()

        def emit_lns(g, spread):
            for j in range(G):
                c = G * g + j
                thunks = ln(c, statsA, xn1[:, 1024 * j:1024 * (j + 1)])

                def first(j=j, c=c, t0=thunks[0]):
                    sx = sX0p[c // 4] if c < 8 else sX1
                    r.wait("scalar", sx, 16)
                    if c >= G:
                        r.wait("scalar", sTN, E[("xtr", (c - G) // G, j)])
                    t0()
                thunks[0] = first
                if spread:
                    for k, fn in enumerate(thunks):
                        defer(1 + 2 * j + k // 3, fn)
                else:
                    for fn in thunks:
                        fn()

        def emit_xtr(g, j, inline=False):
            par = g % 2
            c = G * g + j
            r.wait("tensor", sSC, E[("xn1", c)])
            if g == 0 and j == 0:
                r.wait("tensor", sGP, E[("const",)])
            prev = E.get(("ptfree",))
            if prev:
                r.wait("tensor", sVE, prev)
            for kc in range(8):
                rg(0, 128)
                r.op("tensor", "transpose", pT[:, 128 * kc:128 * (kc + 1)],
                     xn1[:, 1024 * j + 128 * kc:1024 * j + 128 * (kc + 1)], idb[:, :],
                     inc=(sTN if kc == 7 else None))
            E[("xtr", g, j)] = sTN.n

            def evac():
                r.wait("vector", sTN, E[("xtr", g, j)])
                if g >= 2 and j == 0:
                    r.wait("vector", sTN, E[("vmm", G * (g - 2) + 1, 1)])
                xf = xfm[:, :]
                dst = bass.AP(xf.tensor, xf.offset + 2048 * par + 128 * j,
                              [list(xf.ap[0]), [256, 8], [1, 128]])
                E[("ptfree",)] = r.op("vector", "tensor_copy", dst, pT[:, :], inc=sVE)
                if j == 1:
                    E[("xfm", g)] = sVE.n
            if inline:
                evac()
            else:
                defer(1, evac)

        def emit_qk(g, p, inline=False):
            par = g % 2
            if p == 0:
                r.wait("tensor", sVE, E[("xfm", g)])
            bank = pbulk[p % 2]
            wait_ev("tensor", E.get(("pbulkfree", p % 2)))
            is_q = p < 4
            base = 0 if is_q else 1024
            for sub in range(2):
                oc = (2 * p + sub) % 8
                for kc in range(KC):
                    if g == 0 and p == 0 and sub == 0:
                        r.wait("tensor", sWAp[kc // 2], 16)
                    rg(0, 128)
                    r.op("tensor", "matmul",
                         bank[:, 256 * sub:256 * (sub + 1)],
                         wattn[:, 3072 * kc + base + 128 * oc:3072 * kc + base + 128 * (oc + 1)],
                         xfm[:, 2048 * par + 256 * kc:2048 * par + 256 * (kc + 1)],
                         start=(kc == 0 and sub == 0), stop=(kc == KC - 1 and sub == 1),
                         inc=(sTN if (kc == KC - 1 and sub == 1) else None))
            E[("qkmm", g, p)] = sTN.n

            def minexp():
                r.wait("scalar", sTN, E[("qkmm", g, p)])
                eprev = E.get(("qkem", g, p - 1)) if p > 0 else (E.get(("qkem", g - 1, 7)) if g > 0 else None)
                if eprev:
                    r.wait("scalar", sVE, eprev)
                E[("qkexp", g, p)] = r.op("scalar", "activation", eluex[:, :],
                                          bank[:, :], AF.Exp, inc=sSC)

            def stt():
                r.wait("vector", sSC, E[("qkexp", g, p)])
                r.drain("vector")
                E[("qkem", g, p)] = r.op("vector", "tensor_scalar", elumin[:, :],
                                         eluex[:, :], 1.0, None, OP.min, inc=sVE)
                if g >= 2 and p == 0:
                    r.wait("vector", sTN, E[("ndmm", G * (g - 2) + 1, NHP - 1)])
                dstbuf = qfm if is_q else kfm
                off = 2048 * par + (512 * p if is_q else 512 * (p - 4))
                r.drain("vector")
                ev = r.op("vector", "scalar_tensor_tensor", dstbuf[:, off:off + 512],
                          bank[:, :], 0.0, elumin[:, :], OP.max, OP.add, inc=sVE)
                E[("pbulkfree", p % 2)] = (sVE, ev)
                E[("qkstt", g, p)] = ev
            if inline:
                minexp()
                stt()
            else:
                defer(1, minexp)
                defer_post(1, stt)

        def emit_v(g, j, oh, inline=False):
            par = g % 2
            c = G * g + j
            bank = pbulk[oh]
            wait_ev("tensor", E.get(("pbulkfree", oh)))
            if g == 0 and j == 0 and oh == 0:
                r.wait("tensor", sGP, E[("vones",)])
            for kc in range(KC):
                rg(0, 128)
                r.op("tensor", "matmul", bank[:, :],
                     xfm[:, 2048 * par + 256 * kc + 128 * j:
                         2048 * par + 256 * kc + 128 * (j + 1)],
                     wattn[:, 3072 * kc + 2048 + 512 * oh:3072 * kc + 2048 + 512 * (oh + 1)],
                     start=(kc == 0), stop=(kc == KC - 1),
                     inc=(sTN if kc == KC - 1 else None))
            E[("vmm", c, oh)] = sTN.n

            def evac():
                r.wait("scalar", sTN, E[("vmm", c, oh)])
                if c >= G and oh == 0:
                    r.wait("scalar", sTN, E[("attn_done", c - G)])
                vv = vaug[:, :]
                dst = bass.AP(vv.tensor, vv.offset + 1040 * j + 65 * 8 * oh,
                              [list(vv.ap[0]), [65, 8], [1, 64]])
                src = bass.AP(bank.ap().tensor, bank.ap().offset,
                              [list(bank.ap().ap[0]), [64, 8], [1, 64]])
                ev = r.op("scalar", "copy", dst, src, inc=sSC)
                E[("pbulkfree", oh)] = (sSC, ev)
                E[("vevS", c, oh)] = ev
            if inline:
                evac()
            else:
                defer(1, evac)

        def emit_ytr(g, j, c, inline=False):
            r.wait("tensor", sVE, E[("yev", c, NHP - 1)])
            prev = E.get(("ptfree",))
            if prev:
                r.wait("tensor", sVE, prev)
            for kc in range(8):
                rg(0, 128)
                r.op("tensor", "transpose", pT[:, 128 * kc:128 * (kc + 1)],
                     ytok[:, 1024 * j + 128 * kc:1024 * j + 128 * (kc + 1)], idb[:, :],
                     inc=(sTN if kc == 7 else None))
            E[("ytr", c)] = sTN.n

            def evac():
                r.wait("vector", sTN, E[("ytr", c)])
                if c >= G:
                    r.wait("vector", sTN, E[("projmm", c - G, 1)])
                yf = yfm[:, :]
                dst = bass.AP(yf.tensor, yf.offset + 128 * j,
                              [list(yf.ap[0]), [256, 8], [1, 128]])
                E[("ptfree",)] = r.op("vector", "tensor_copy", dst, pT[:, :], inc=sVE)
                E[("yfm", c)] = sVE.n
            if inline:
                evac()
            else:
                defer(1, evac)

        def emit_proj(g, j, c, oh, inline=False):
            bank = pbulk[oh]
            wait_ev("tensor", E.get(("pbulkfree", oh)))
            if c == 0 and oh == 0:
                r.wait("tensor", sWP, 16)
            r.wait("tensor", sVE, E[("yfm", c)])
            for kc in range(KC):
                rg(0, 128)
                r.op("tensor", "matmul", bank[:, :],
                     yfm[:, 256 * kc + 128 * j:256 * kc + 128 * (j + 1)],
                     wproj[:, 1024 * kc + 512 * oh:1024 * kc + 512 * (oh + 1)],
                     start=(kc == 0), stop=(kc == KC - 1),
                     inc=(sTN if kc == KC - 1 else None))
            E[("projmm", c, oh)] = sTN.n

            def resid():
                r.wait("vector", sTN, E[("projmm", c, oh)])
                r.drain("vector")
                ev = r.op("vector", "tensor_add",
                          xres[:, 1024 * c + 512 * oh:1024 * c + 512 * (oh + 1)],
                          xres[:, 1024 * c + 512 * oh:1024 * c + 512 * (oh + 1)],
                          bank[:, :], inc=sVE)
                E[("pbulkfree", oh)] = (sVE, ev)
                if oh == 1:
                    E[("res1", c)] = ev
            if inline:
                resid()
            else:
                defer(1, resid)

        def emit_att(g, j, c, hp):
            par = g % 2
            slot = hp % 2
            qk_cols = slice(2048 * par + 256 * hp + 128 * j,
                            2048 * par + 256 * hp + 128 * (j + 1))
            if hp > 0:
                pc, php, pj = c, hp - 1, j
            elif c > 0:
                pc, php, pj = c - 1, 7, (c - 1) % G
            else:
                pc = None
            if hp >= 2:
                p2c, p2hp = c, hp - 2
            elif c > 0:
                p2c, p2hp = c - 1, hp + 6
            else:
                p2c = None
            r.wait("tensor", sVE, E[("qkstt", g, hp // 2)])
            r.wait("tensor", sVE, E[("qkstt", g, 4 + hp // 2)])
            prev = E.get(("atlo", p2c, p2hp)) if p2c is not None else None
            if prev:
                r.wait("tensor", sVE, prev)
            rg(0, 64)
            E[("st1", c, hp)] = r.op("tensor", "matmul",
                                     pSTg[0][:, 128 * slot:128 * (slot + 1)],
                                     kfm[0:64, qk_cols], qfm[0:64, qk_cols],
                                     start=True, stop=True, inc=sTN)
            pn = E.get(("ndmm", c, hp - 2))
            if pn is None and c > 0:
                pn = E.get(("ndmm", c - 1, hp + 6))
            r.wait("vector", sTN, E[("st1", c, hp)])
            if pn:
                r.wait("vector", sTN, pn)
            E[("atlo", c, hp)] = r.op(
                "vector", "tensor_tensor", at_sb[:, 256 * slot:256 * slot + 128],
                pSTg[0][:, 128 * slot:128 * (slot + 1)], maskT[:, 0:128],
                OP.mult, inc=sVE)
            kprev = E.get(("ktok", pc, php)) if pc is not None else None
            if kprev:
                r.wait("tensor", sSC, kprev)
            rg(0, 128)
            E[("kt", c, hp)] = r.op("tensor", "transpose", pKT[:, :],
                                    kfm[:, qk_cols], idb[:, :], inc=sTN)
            prevh = E.get(("at", p2c, p2hp)) if p2c is not None else None
            if prevh:
                r.wait("tensor", sVE, prevh)
            rg(64, 64)
            E[("st2", c, hp)] = r.op("tensor", "matmul",
                                     pSTg[1][:, 128 * slot:128 * (slot + 1)],
                                     kfm[64:128, qk_cols], qfm[64:128, qk_cols],
                                     start=True, stop=True, inc=sTN)
            r.wait("vector", sTN, E[("st2", c, hp)])
            if pn:
                r.wait("vector", sTN, pn)
            E[("at", c, hp)] = r.op(
                "vector", "tensor_tensor",
                at_sb[:, 256 * slot + 128:256 * (slot + 1)],
                pSTg[1][:, 128 * slot:128 * (slot + 1)], maskT[:, 128:256],
                OP.mult, inc=sVE)
            if pc is not None:
                emit_ds(pc, php, pj)
            else:
                rg(0, 128)
                r.op("tensor", "matmul", pDS[0:64, 0:1], idb[:, 0:64],
                     idb[:, 0:1], start=True, stop=True)
            sb_prev = E.get(("sbf", c - 1, hp))
            if sb_prev:
                r.wait("tensor", sSC, sb_prev)
            elif c == 0:
                r.wait("tensor", sGP, E[("const",)])
            yprev = E.get(("ndcp", pc, php)) if pc is not None else None
            if yprev:
                r.wait("tensor", sSC, yprev)
            r.wait("tensor", sSC, E[("vevS", c, 0 if hp < 4 else 1)])
            rg(0, 64)
            r.op("tensor", "matmul", pND[:, 0:65],
                 qfm[0:64, qk_cols], saugbf[0:64, 65 * hp:65 * (hp + 1)],
                 start=True, stop=False)
            r.wait("tensor", sVE, E[("atlo", c, hp)])
            rg(0, 128)
            r.op("tensor", "matmul", pND[:, 0:65],
                 at_sb[:, 256 * slot:256 * slot + 128],
                 vaug[:, 1040 * j + 65 * 2 * hp:1040 * j + 65 * (2 * hp + 1)],
                 start=False, stop=False)
            rg(64, 64)
            r.op("tensor", "matmul", pND[:, 65:130],
                 qfm[64:128, qk_cols], saugbf[64:128, 65 * hp:65 * (hp + 1)],
                 start=False, stop=False)
            r.wait("tensor", sVE, E[("at", c, hp)])
            rg(0, 128)
            E[("ndmm", c, hp)] = r.op(
                "tensor", "matmul", pND[:, 65:130],
                at_sb[:, 256 * slot + 128:256 * (slot + 1)],
                vaug[:, 1040 * j + 65 * (2 * hp + 1):1040 * j + 65 * (2 * hp + 2)],
                start=False, stop=True, inc=sTN)
            r.wait("scalar", sTN, E[("kt", c, hp)])
            dsprev = E.get(("dsmm", c - 1 if hp <= 1 else c, hp - 2 if hp > 1 else hp + 6))
            if dsprev:
                r.wait("scalar", sTN, dsprev)
            E[("ktok", c, hp)] = r.op(
                "scalar", "copy", ktok[:, 128 * slot:128 * (slot + 1)], pKT[:, :], inc=sSC)
            r.wait("scalar", sTN, E[("ndmm", c, hp)])
            nds = 130 * (hp % 2)
            E[("ndcp", c, hp)] = r.op("scalar", "copy",
                                      ndbuf[:, nds:nds + 130], pND[:, :], inc=sSC)

            def norm():
                r.wait("vector", sSC, E[("ndcp", c, hp)])
                r.drain("vector")
                ndr = ndbuf[:, nds:nds + 130].rearrange("p (h c2) -> p h c2", h=2)
                r.op("vector", "tensor_scalar", r2pre[:, :], ndr[:, :, 64],
                     ATT_EPS, None, OP.add)
                r.drain("vector")
                r.op("vector", "reciprocal", r2[:, :], r2pre[:, :])
                r.drain("vector")
                r2ap = r2[:, :]
                r2e = bass.AP(r2ap.tensor, r2ap.offset, [list(r2ap.ap[0]), [1, 2], [0, 64]])
                if c >= G and hp == 0:
                    r.wait("vector", sTN, E[("ytr", c - G)])
                E[("yev", c, hp)] = r.op(
                    "vector", "tensor_tensor",
                    ytok[:, 1024 * j + 128 * hp:1024 * j + 128 * (hp + 1)],
                    ndr[:, :, 0:64], r2e, OP.mult, inc=sVE)
            defer_post(1, norm)

        emit_lns(0, spread=False)
        for jj in range(G):
            emit_xtr(0, jj, inline=True)
        for pp in range(8):
            emit_qk(0, pp, inline=True)
        emit_v(0, 0, 0, inline=True)
        emit_v(0, 0, 1, inline=True)

        LN2APPLY = {}
        xn2ref = [None]
        for g in range(NG):
            c0, c1 = G * g, G * g + 1
            nxt = g + 1 < NG
            pc0, pc1 = c0 - 2, c1 - 2
            sched = {i: [] for i in range(16)}
            if nxt:
                sched[0].append(lambda g=g: emit_lns(g + 1, spread=True))
            sched[0].append(lambda g=g: emit_v(g, 1, 0))
            sched[1].append(lambda g=g: emit_v(g, 1, 1))
            if g > 0:
                sched[0].append(lambda g=g, pc0=pc0: emit_ytr(g - 1, 0, pc0))
                sched[2].append(lambda g=g, pc0=pc0: emit_proj(g - 1, 0, pc0, 0))
                sched[2].append(lambda g=g, pc0=pc0: emit_proj(g - 1, 0, pc0, 1))
                sched[3].append(lambda g=g, pc1=pc1: emit_ytr(g - 1, 1, pc1))
                sched[4].append(lambda g=g, pc1=pc1: emit_proj(g - 1, 1, pc1, 0))
                sched[4].append(lambda g=g, pc1=pc1: emit_proj(g - 1, 1, pc1, 1))
            if nxt:
                sched[4].append(lambda g=g: emit_xtr(g + 1, 0))
                sched[6].append(lambda g=g: emit_xtr(g + 1, 1))
                for i in range(8):
                    sched[7 + i].append(lambda g=g, p=i: emit_qk(g + 1, p))
                sched[15].append(lambda g=g: emit_v(g + 1, 0, 0))
            if g > 0:
                for ci in range(2):
                    pc = 2 * (g - 1) + ci
                    pslot = pc % 2
                    th = ln(pc, statsB,
                            (lambda pslot=pslot: xn2ref[0][:, 1024 * pslot:1024 * (pslot + 1)]),
                            vec_apply=True)
                    LN2APPLY[pc] = th[-1]

                    def wrap0(pc=pc, t0=th[0]):
                        r.wait("scalar", sVE, E[("res1", pc)])
                        t0()
                    base = 8 + 4 * ci

                    def pack(fns):
                        def f():
                            for fn in fns:
                                fn()
                        return f
                    sched[base].append(wrap0)
                    sched[base + 1].append(th[1])
                    sched[base + 2].append(pack(th[2:4]))
                    sched[base + 3].append(pack(th[4:8]))
            for blk in range(16):
                j = blk // 8
                c = c0 if blk < 8 else c1
                hp = blk % 8
                flush()
                if blk == 0 and g > 0:
                    emit_v(g, 0, 1)
                emit_att(g, j, c, hp)
                flush_post()
                for fn in sched[blk]:
                    fn()
                BI["n"] += 1
            cut(f"A{g}")
        flush_all()
        emit_ds(TC - 1, NHP - 1, 1)
        flush_all()
        emit_ytr(NG - 1, 0, TC - 2, inline=True)
        emit_proj(NG - 1, 0, TC - 2, 0, inline=True)
        emit_proj(NG - 1, 0, TC - 2, 1, inline=True)
        emit_ytr(NG - 1, 1, TC - 1, inline=True)
        emit_proj(NG - 1, 1, TC - 1, 0, inline=True)
        emit_proj(NG - 1, 1, TC - 1, 1, inline=True)

        for pc in (TC - 2, TC - 1):
            th = ln(pc, statsB,
                    (lambda pslot=pc % 2: xn2ref[0][:, 1024 * pslot:1024 * (pslot + 1)]),
                    vec_apply=True)
            LN2APPLY[pc] = th[-1]
            r.wait("scalar", sVE, E[("res1", pc)])
            for fn in th[:-1]:
                fn()

        r.wait("tensor", sVE, E[("sadd", TC - 1, NHP - 1)])
        for dcol in range(24):
            rg(0, 128)
            r.op("tensor", "matmul", pDS[0:64, dcol:dcol + 1], idb[:, 0:64],
                 idb[:, dcol:dcol + 1], start=True, stop=True)

        E[("phaseA_tn",)] = sTN.n
        E[("phaseA_ve",)] = sVE.n
        E[("phaseA_sc",)] = sSC.n
        E[("phaseA_gp",)] = sGP.n

      with ExitStack() as bctx:
          bsb = lambda name, shape, dt: bctx.enter_context(nc.sbuf_tensor(name, shape, dt))
          bps = lambda name, shape, dt: bctx.enter_context(nc.psum_tensor(name, shape, dt))

          xn2fm = bsb("xn2fm", [128, KC * 2048], BF16)
          acc = bsb("acc", [128, 4 * 2048], F32)
          ewt = bsb("ewt", [128, 2 * 4096], BF16)
          ewn = bsb("ewn", [128, 2 * 4096], BF16)
          xn2tok = bsb("xn2tok", [128, 2 * 1024], BF16)
          xn2ref[0] = xn2tok
          gbuf = bsb("gbuf", [128, 16 * 512], BF16)

          pT2 = [bps("pT2a", [128, 512], BF16), bps("pT2b", [128, 512], BF16)]
          pE = [bps("pE0", [128, 512], F32), bps("pE1", [128, 512], F32)]
          gctx = ExitStack()
          gps = lambda name, shape, dt: gctx.enter_context(nc.psum_tensor(name, shape, dt))
          pPL = [gps("pPL0", [1, 512], F32), gps("pPL1", [1, 512], F32)]
          pG = gps("pG", [128, 8], F32)
          pFs = [None, None]

          for eng in ("tensor", "vector", "scalar", "gpsimd"):
              r.wait(eng, sTN, E[("phaseA_tn",)])
              r.wait(eng, sVE, E[("phaseA_ve",)])
              r.wait(eng, sSC, E[("phaseA_sc",)])
              r.wait(eng, sGP, E[("phaseA_gp",)])

          def emit_ew_load(oh, e, idx):
              buf = idx % 2
              if idx >= 2:
                  r.wait("gpsimd", sTN, E[("ewtr", idx - 2)])
              src = ew_d[e, 512 * oh:512 * (oh + 1), :].rearrange("(oc p) d -> p oc d", p=128)
              dst = ewn[:, 4096 * buf:4096 * (buf + 1)].rearrange("p (oc d) -> p oc d", d=1024)
              E[("ewld", idx)] = r.op("gpsimd", "dma_start", inc=sEW[buf], k=16, out=dst, in_=src)

          def emit_ewtr_group(idx, oc4):
              buf = idx % 2
              if oc4 == 0:
                  r.wait("tensor", sEW[buf], E[("ewld", idx)])
              for b in range(2):
                  prev = E.get(("pt2free", b))
                  if prev:
                      r.wait("tensor", sVE, prev)
                  for i in range(4):
                      kc = 4 * b + i
                      rg(0, 128)
                      r.op("tensor", "transpose",
                           pT2[b][:, 128 * i:128 * (i + 1)],
                           ewn[:, 4096 * buf + 1024 * oc4 + 128 * kc:
                               4096 * buf + 1024 * oc4 + 128 * (kc + 1)],
                           idb[:, :], inc=(sTN if i == 3 else None))
                  E[("ewtrb", idx, oc4, b)] = sTN.n
                  r.wait("vector", sTN, E[("ewtrb", idx, oc4, b)])
                  ew = ewt[:, :]
                  dst = bass.AP(ew.tensor,
                                ew.offset + 4096 * buf + 512 * (4 * b) + 128 * oc4,
                                [list(ew.ap[0]), [512, 4], [1, 128]])
                  E[("pt2free", b)] = r.op("vector", "tensor_copy", dst,
                                           pT2[b][:, :], inc=sVE)
              E[("ewtevq", idx, oc4)] = sVE.n
              if oc4 == 3:
                  E[("ewtr", idx)] = sTN.n
                  E[("ewtev", idx)] = sVE.n

          def emit_ewtr(idx):
              for oc4 in range(4):
                  emit_ewtr_group(idx, oc4)

          TID = {"n": 0}
          PSTT = []

          def emit_etile(idx, oh, e, ts, oc, defer_stt):
              buf = idx % 2
              tid = TID["n"]
              TID["n"] += 1
              pb = pE[tid % 2]
              prevg = E.get(("geluk", tid - 2))
              if prevg is not None:
                  r.wait("tensor", sSC, prevg)
              r.wait("tensor", sVE, E[("ewtevq", idx, oc)])
              r.wait("tensor", sVE, E[("x2e", 4 * ts + 3)])
              for kc in range(KC):
                  rg(0, 128)
                  r.op("tensor", "matmul", pb[:, :],
                       ewt[:, 4096 * buf + 512 * kc + 128 * oc:
                           4096 * buf + 512 * kc + 128 * (oc + 1)],
                       xn2fm[:, 2048 * kc + 512 * ts:2048 * kc + 512 * (ts + 1)],
                       start=(kc == 0), stop=(kc == KC - 1),
                       inc=(sTN if kc == KC - 1 else None))
              e_mm = sTN.n
              r.wait("scalar", sTN, e_mm)
              gslot = tid % 16
              prevs = E.get(("sttk", tid - 16))
              if prevs is not None:
                  r.wait("scalar", sVE, prevs)
              if tid == 0:
                  r.wait("scalar", sVE, E[("ebev", KC - 1)])
              E[("geluk", tid)] = r.op(
                  "scalar", "activation", gbuf[:, 512 * gslot:512 * (gslot + 1)],
                  pb[:, :], AF.Gelu,
                  bias=expbfm[:, 8 * (4 * oh + oc) + e:8 * (4 * oh + oc) + e + 1],
                  inc=sSC)

              def stt():
                  r.wait("vector", sSC, E[("geluk", tid)])
                  if tid == 0:
                      r.wait("vector", sVE, E[("gates",)])
                  r.drain("vector")
                  if idx == 8:
                      r.wait("vector", sTN, E[("ftr", 0, 4 * ts + 3)])
                  aslice = acc[:, 2048 * oc + 512 * ts:2048 * oc + 512 * (ts + 1)]
                  if e == 0:
                      E[("sttk", tid)] = r.op(
                          "vector", "tensor_scalar", aslice,
                          gbuf[:, 512 * gslot:512 * (gslot + 1)],
                          gatesb[:, e:e + 1], None, OP.mult, inc=sVE)
                  else:
                      E[("sttk", tid)] = r.op(
                          "vector", "scalar_tensor_tensor", aslice,
                          gbuf[:, 512 * gslot:512 * (gslot + 1)],
                          gatesb[:, e:e + 1], aslice, OP.mult, OP.add, inc=sVE)
                  E[("acc", idx, ts, oc)] = E[("sttk", tid)]
              if defer_stt:
                  PSTT.append(stt)
              else:
                  stt()

          emit_ew_load(0, 0, 0)
          emit_ew_load(0, 1, 1)

          r.wait("tensor", sGWl, 16)
          for kc in range(KC):
              rg(0, 8)
              e_t = r.op("tensor", "transpose", pG[:, :],
                         gw_s[0:8, 128 * kc:128 * (kc + 1)], idf[0:8, 0:8], inc=sTN)
              r.wait("vector", sTN, e_t)
              E[("gwev", kc)] = r.op("vector", "tensor_copy", gwT[:, 8 * kc:8 * (kc + 1)],
                                     pG[:, :], inc=sVE)
              r.wait("tensor", sVE, E[("gwev", kc)])
          r.wait("tensor", sEBl, 16)
          for oc in range(KC):
              rg(0, 8)
              e_t = r.op("tensor", "transpose", pG[:, :],
                         eb_s[0:8, 128 * oc:128 * (oc + 1)], idf[0:8, 0:8], inc=sTN)
              r.wait("vector", sTN, e_t)
              E[("ebev", oc)] = r.op("vector", "tensor_copy", expbfm[:, 8 * oc:8 * (oc + 1)],
                                     pG[:, :], inc=sVE)
              r.wait("tensor", sVE, E[("ebev", oc)])

          for c in range(TC):
              slot = c % 2
              if c >= 2:
                  r.wait("vector", sTN, E[("xtr2", c - 2)])
              if c in LN2APPLY:
                  LN2APPLY[c]()
              else:
                  for _t in ln(c, statsB, xn2tok[:, 1024 * slot:1024 * (slot + 1)],
                               vec_apply=True):
                      _t()
              E[("xn2", c)] = E[("xn1", c)]
              r.wait("tensor", sVE, E[("xn2", c)])
              for b in range(2):
                  prev = E.get(("pt2free", b))
                  if prev:
                      r.wait("tensor", sVE, prev)
                  for i in range(4):
                      kc = 4 * b + i
                      rg(0, 128)
                      r.op("tensor", "transpose", pT2[b][:, 128 * i:128 * (i + 1)],
                           xn2tok[:, 1024 * slot + 128 * kc:1024 * slot + 128 * (kc + 1)],
                           idb[:, :], inc=(sTN if i == 3 else None))
                  E[("xtr2b", c, b)] = sTN.n
                  r.wait("vector", sTN, E[("xtr2b", c, b)])
                  xf2 = xn2fm[:, :]
                  dst = bass.AP(xf2.tensor, xf2.offset + 2048 * (4 * b) + 128 * c,
                                [list(xf2.ap[0]), [2048, 4], [1, 128]])
                  E[("pt2free", b)] = r.op("vector", "tensor_copy", dst, pT2[b][:, :], inc=sVE)
              E[("xtr2", c)] = sTN.n
              E[("x2e", c)] = sVE.n
              s0 = 8 * c
              for half in range(2):
                  rg(0, 128)
                  r.op("tensor", "matmul", pPL[half][0:1, :],
                       statsB[:, s0 + 5:s0 + 6],
                       xres[:, 1024 * c + 512 * half:1024 * c + 512 * (half + 1)],
                       start=(c == 0), stop=(c == TC - 1),
                       inc=(sTN if c == TC - 1 else None))
              if c == 4:
                  emit_ewtr(0)
              elif c == 5:
                  emit_ewtr(1)
              elif c in (6, 7):
                  for _oc in (0, 1) if c == 6 else (2, 3):
                      emit_etile(0, 0, 0, 0, _oc, True)
              elif c in (8, 9):
                  for _oc in (0, 1) if c == 8 else (2, 3):
                      emit_etile(0, 0, 0, 1, _oc, True)
              elif c in (12, 13):
                  for _oc in (0, 1) if c == 12 else (2, 3):
                      emit_etile(0, 0, 0, 2, _oc, True)
          E[("plmm",)] = sTN.n
          E[("xn2fm_done",)] = sVE.n
          cut("Bpro")

          stb = statsB[:, :]
          bias_cols = bass.AP(stb.tensor, stb.offset + 6, [list(stb.ap[0]), [8, TC]])
          r.drain("vector")
          e_rs = r.op("vector", "reduce_sum", pooledfm[:, 0:1], bias_cols, axis=AX, inc=sVE)
          r.wait("tensor", sVE, e_rs)
          rg(0, 128)
          e_bs = r.op("tensor", "matmul", pG[0:1, 0:1], pooledfm[:, 0:1], onesf[:, 0:1],
                      start=True, stop=True, inc=sTN)
          emit_etile(0, 0, 0, 3, 0, True)
          r.wait("vector", sTN, e_bs)
          e_bsv = r.op("vector", "tensor_copy", small[0:1, 1:2], pG[0:1, 0:1], inc=sVE)
          r.wait("vector", sTN, E[("plmm",)])
          r.drain("vector")
          for half in range(2):
              r.op("vector", "tensor_scalar", pooled_sb[0:1, 512 * half:512 * (half + 1)],
                   pPL[half][0:1, :], small[0:1, 1:2], 1.0 / 2048, OP.add, OP.mult)
          r.drain("vector")
          e_pool = r.op("vector", "tensor_copy", small[0:1, 2:3], pooled_sb[0:1, 0:1], inc=sVE)
          r.wait("tensor", sVE, e_pool)
          for kc in range(KC):
              rg(0, 1)
              r.op("tensor", "transpose", pG[:, kc:kc + 1],
                   pooled_sb[0:1, 128 * kc:128 * (kc + 1)], idf[0:1, 0:1],
                   inc=(sTN if kc == KC - 1 else None))
          E[("plfm",)] = sTN.n
          emit_etile(0, 0, 0, 3, 1, True)
          r.wait("vector", sTN, E[("plfm",)])
          e_plfm = r.op("vector", "tensor_copy", pooledfm[:, :], pG[:, :], inc=sVE)
          r.wait("tensor", sVE, e_plfm)
          for kc in range(KC):
              rg(0, 128)
              r.op("tensor", "matmul", pPL[0][0:1, 0:8],
                   pooledfm[:, kc:kc + 1], gwT[:, 8 * kc:8 * (kc + 1)],
                   start=(kc == 0), stop=(kc == KC - 1),
                   inc=(sTN if kc == KC - 1 else None))
          E[("lgmm",)] = sTN.n
          emit_etile(0, 0, 0, 3, 2, True)
          r.wait("vector", sTN, E[("lgmm",)])
          r.wait("vector", sGBl, 16)
          r.drain("vector")
          r.op("vector", "tensor_add", lg_sb[0:1, :], pPL[0][0:1, 0:8], gb_s[0:1, :])
          r.drain("vector")
          r.op("vector", "reduce_max", small[0:1, 4:5], lg_sb[0:1, :], axis=AX, negate=True)
          r.drain("vector")
          r.op("vector", "tensor_scalar", small[0:1, 5:6], small[0:1, 4:5], -1.0, None, OP.mult)
          r.drain("vector")
          r.op("vector", "tensor_scalar", tt_sb[0:1, 0:8], lg_sb[0:1, :], small[0:1, 5:6],
               None, OP.is_equal)
          r.drain("vector")
          r.op("vector", "scalar_tensor_tensor", tt_sb[0:1, 8:16], tt_sb[0:1, 0:8], 1e30,
               lg_sb[0:1, :], OP.mult, OP.subtract)
          r.drain("vector")
          r.op("vector", "tensor_reduce", small[0:1, 6:7], tt_sb[0:1, 8:16], axis=AX, op=OP.min)
          r.drain("vector")
          e_m2 = r.op("vector", "tensor_scalar", small[0:1, 7:8], small[0:1, 6:7],
                      -1.0, None, OP.mult, inc=sVE)
          r.wait("scalar", sVE, e_m2)
          e_ex = r.op("scalar", "activation", tt_sb[0:1, 8:16], lg_sb[0:1, :], AF.Exp,
                      bias=small[0:1, 4:5], inc=sSC)
          r.wait("vector", sSC, e_ex)
          r.drain("vector")
          r.op("vector", "tensor_scalar", tt_sb[0:1, 0:8], lg_sb[0:1, :], small[0:1, 7:8],
               None, OP.is_ge)
          r.drain("vector")
          r.op("vector", "tensor_mul", tt_sb[0:1, 0:8], tt_sb[0:1, 0:8], tt_sb[0:1, 8:16])
          r.drain("vector")
          r.op("vector", "reduce_sum", small[0:1, 8:9], tt_sb[0:1, 0:8], axis=AX)
          r.drain("vector")
          r.op("vector", "reciprocal", small[0:1, 9:10], small[0:1, 8:9])
          r.drain("vector")
          e_gate = r.op("vector", "tensor_scalar", tt_sb[0:1, 0:8], tt_sb[0:1, 0:8],
                        small[0:1, 9:10], None, OP.mult, inc=sVE)
          r.wait("tensor", sVE, e_gate)
          rg(0, 1)
          e_gb = r.op("tensor", "matmul", pG[:, :], onesf[0:1, :], tt_sb[0:1, 0:8],
                      start=True, stop=True, inc=sTN)
          emit_etile(0, 0, 0, 3, 3, True)
          r.wait("vector", sTN, e_gb)
          E[("gates",)] = r.op("vector", "tensor_copy", gatesb[:, :], pG[:, :], inc=sVE)
          cut("Bgates")

          for fn in PSTT:
              fn()
          PSTT.clear()
          gctx.close()
          pFs[0] = bps("pF", [128, 512], F32)
          pFs[1] = bps("pF2", [128, 512], F32)

          def emit_final_chunk(oh, c):
                  ts, u = c // 4, c % 4
                  fp = (16 * oh + c) % 2
                  pFb = pFs[fp]
                  r.wait("tensor", sVE, E[("acc", 8 * oh + 7, ts, 3)])
                  prevf = E.get(("ffp", fp))
                  if prevf:
                      r.wait("tensor", sVE, prevf)
                  for oc in range(4):
                      rg(0, 128)
                      r.op("tensor", "transpose", pFb[:, 128 * oc:128 * (oc + 1)],
                           acc[:, 2048 * oc + 512 * ts + 128 * u:
                               2048 * oc + 512 * ts + 128 * (u + 1)],
                           idf[:, :], inc=(sTN if oc == 3 else None))
                  E[("ftr", oh, c)] = sTN.n
                  r.wait("vector", sTN, E[("ftr", oh, c)])
                  r.drain("vector")
                  E[("ffp", fp)] = r.op(
                      "vector", "tensor_add",
                      xres[:, 1024 * c + 512 * oh:1024 * c + 512 * (oh + 1)],
                      xres[:, 1024 * c + 512 * oh:1024 * c + 512 * (oh + 1)],
                      pFb[:, :], inc=sVE)
                  E[("fin", oh, c)] = sVE.n

          def emit_store(oh):
              for half in range(2):
                  for c in range(8 * half, 8 * (half + 1)):
                      r.wait("sync", sVE, E[("fin", oh, c)])
                  r.op(
                      "sync", "dma_start", inc=sST, k=16,
                      out=out_d[1024 * half:1024 * (half + 1),
                                512 * oh:512 * (oh + 1)].rearrange("(c p) d -> p c d", p=128),
                      in_=bass.AP(xres[:, :].tensor,
                                  xres[:, :].offset + 8192 * half + 512 * oh,
                                  [list(xres[:, :].ap[0]), [1024, 8], [1, 512]]))

          emit_ew_load(0, 2, 2)
          for idx in range(1, 16):
              oh, e = idx // 8, idx % 8
              if idx + 2 < 16:
                  n_idx = idx + 2
                  emit_ew_load(n_idx // 8, n_idx % 8, n_idx)
              for ts in range(4):
                  if idx == 8:
                      for u in range(4):
                          emit_final_chunk(0, 4 * ts + u)
                  for oc in range(4):
                      emit_etile(idx, oh, e, ts, oc, False)
                  if 2 <= idx + 1 < 16:
                      emit_ewtr_group(idx + 1, ts)
                  if idx == 15:
                      for u in range(4):
                          emit_final_chunk(1, 4 * ts + u)
              if idx == 8:
                  emit_store(0)
          emit_store(1)

          r.wait("sync", sST, sST.n)

    except _Cut:
        r.op("sync", "dma_start", inc=sST, k=16,
             out=out_d[:, :].rearrange("(c p) d -> p c d", p=128),
             in_=xres[:, :].rearrange("p (c d) -> p c d", d=1024))
        r.wait("sync", sST, sST.n)
        _replay(nc, r)
        return nc
    _replay(nc, r)
    ctx.close()
    return nc


def _replay(nc, r):
    with nc.Block() as block:
        def make_body(eng_name):
            def body(eng):
                for meth, args, kw, inc, k in r.q[eng_name]:
                    res = getattr(eng, meth)(*args, **kw)
                    if inc is not None:
                        res.then_inc(inc.h, k)
            return body
        block.tensor(make_body("tensor"))
        block.vector(make_body("vector"))
        block.scalar(make_body("scalar"))
        block.gpsimd(make_body("gpsimd"))
        block.sync(make_body("sync"))



_NC_CACHE = {}


def kernel(L0_attn_x, L0_attn_w_attn, L0_attn_w_proj,
           L0_ffn_gate_w, L0_ffn_gate_b, L0_ffn_expert_w, L0_ffn_expert_b):
    x = np.ascontiguousarray(np.asarray(L0_attn_x, dtype=np.float32))
    wa = np.ascontiguousarray(np.asarray(L0_attn_w_attn, dtype=np.float32))
    wp = np.ascontiguousarray(np.asarray(L0_attn_w_proj, dtype=np.float32))
    gw = np.ascontiguousarray(np.asarray(L0_ffn_gate_w, dtype=np.float32))
    gb = np.ascontiguousarray(np.asarray(L0_ffn_gate_b, dtype=np.float32)).reshape(1, 8)
    ew = np.ascontiguousarray(np.asarray(L0_ffn_expert_w, dtype=np.float32))
    eb = np.ascontiguousarray(np.asarray(L0_ffn_expert_b, dtype=np.float32))

    if "nc" not in _NC_CACHE:
        _NC_CACHE["nc"] = build_program()
    nc = _NC_CACHE["nc"]
    in_maps = [
        {"x": x[b], "w_attn": wa, "w_proj": wp, "gate_w": gw,
         "gate_b": gb, "expert_w": ew, "expert_b": eb}
        for b in range(8)
    ]
    res = run_bass_kernel_spmd(nc, in_maps, core_ids=list(range(8)))
    out = np.stack([res.results[b]["out"] for b in range(8)], axis=0)
    return out.astype(np.float32)
